# version 17
# baseline (speedup 1.0000x reference)
import sys
for _p in ("/opt/trn_rl_repo",):
    if _p not in sys.path:
        sys.path.append(_p)
import numpy as np
import ml_dtypes
from contextlib import ExitStack

import concourse.bass as bass
import concourse.mybir as mybir
from concourse.bass_utils import run_bass_kernel_spmd
from concourse.aot_env import allow_ici_all_to_all

F32 = mybir.dt.float32
BF16 = mybir.dt.bfloat16
AF = mybir.ActivationFunctionType
ALU = mybir.AluOpType

NCORES = 8
T = 8192
TC = T // NCORES
D = 2048
NKV, QM, HD = 8, 4, 64
INTER = 5632
NIT = INTER // 128
NDT = D // 128
NOC = TC // 128
NTB = T // 512
NQB = T // 1024
NOB = 32
OB_SPLIT = 22

VE_CH0 = 9
VE_CHK = 15
PE_CHK = 7
POP = 6
GS_N = NOB * 32


class Sem:
    def __init__(self, ctx, nc, name):
        self.h = ctx.enter_context(nc.semaphore(name))
        self.n = 0


def masked_pi(qb, sp):
    pi = sp - 4 * qb + 2
    return pi if 0 <= pi < 6 else None


def _ffn_schedule():
    items = []
    for it in range(NIT):
        av_g = it * 12
        for k in range(32):
            items.append(["g", it, k, av_g])
        for k in range(32):
            items.append(["u", it, k, None])
    for it2 in range(32):
        if it2 < 16:
            av = 712 + it2 * 11
        else:
            av = max(916, 712 + it2 * 11)
        for k in range(NIT):
            items.append(["d", it2, k, av])

    sched = [[] for _ in range(GS_N)]
    silu_at = {}
    hdn_at = {}
    act2_at = {}
    qi = 0
    for gs in range(GS_N):
        budget = POP
        while qi < len(items) and budget > 0:
            kind, idx, k, av = items[qi]
            if av is None or av > gs:
                break
            sched[gs].append(qi)
            if kind == "g" and k == 31:
                sgs = gs + 1
                silu_at.setdefault(sgs, []).append(idx)
                for q2 in range(qi + 1, qi + 33):
                    if items[q2][0] == "u" and items[q2][1] == idx:
                        items[q2][3] = sgs + 1
            if kind == "u" and k == 31:
                hdn_at.setdefault(gs + 1, []).append(idx)
                for q2 in range(qi + 1, min(qi + 33, len(items))):
                    if items[q2][0] == "g" and items[q2][1] == idx + 1:
                        items[q2][3] = max(items[q2][3], gs + 2)
            if kind == "d" and k == NIT - 1:
                act2_at.setdefault(gs + 2, []).append(idx)
            qi += 1
            budget -= 1
    tail = list(range(qi, len(items)))
    tail_acts = [items[i][1] for i in tail
                 if items[i][0] == "d" and items[i][2] == NIT - 1]
    lim = OB_SPLIT * 32
    for i, (kind, idx, k, av) in enumerate(items):
        if kind in ("g", "u"):
            assert i < qi, f"gate/up item {i} unscheduled"
    for gdict in (silu_at, hdn_at):
        for gk in gdict:
            assert gk < lim, f"gate/up followup at gs {gk} >= {lim}"
    for gs in range(lim, GS_N):
        for i in sched[gs]:
            assert items[i][0] == "d"
    for gk in act2_at:
        assert gk < GS_N + 64
    return items, sched, silu_at, hdn_at, act2_at, tail, tail_acts


def build_nc(dbg=False):
    nc = bass.Bass("TRN2", target_bir_lowering=False)

    def din(name, shape, dt=F32):
        return nc.dram_tensor(name, list(shape), dt, kind="ExternalInput").ap()

    x_c = din("x_c", [TC, D])
    xT_c = din("xT_c", [D, TC])
    cosg = din("cosg", [128, T])
    sing = din("sing", [128, T])
    masks = din("masks", [128, 2432], BF16)
    wkv_r = din("wkv_r", [16, 128, 128])
    wq_r = din("wq_r", [16, 128, 256])
    wout_r = din("wout_r", [16, 128, 16, 128])
    gw_r = din("gw_r", [NIT, 128, 16, 128])
    uw_r = din("uw_r", [NIT, 128, 16, 128])
    dw_r = din("dw_r", [16, 128, NIT, 128])
    gsc = din("gsc", [128, NIT])
    usc = din("usc", [128, NIT])
    dsc = din("dsc", [128, 16])
    sink = din("sink", [1, 4])
    pswap = din("pswap", [128, 128], BF16)
    ident = din("ident", [128, 128], BF16)
    onesr = din("onesr", [1, 128])
    konst = din("konst", [128, 4])

    out_d = nc.dram_tensor("out_d", [D, TC], F32, kind="ExternalOutput").ap()

    stage_xn = nc.dram_tensor("stage_xn", [NDT * NOC * 128, 128], BF16).ap()
    xn_all = nc.dram_tensor("xn_all", [NCORES * NDT * NOC * 128, 128], BF16,
                            addr_space="Shared").ap()
    q_d = nc.dram_tensor("q_d", [128, 2 * T], BF16).ap()
    hdn_d = nc.dram_tensor("hdn_d", [NIT, 128, 1024], BF16).ap()
    ctx_stage = nc.dram_tensor("ctx_stage", [D, TC], BF16).ap()
    ctx_all = nc.dram_tensor("ctx_all", [D, TC], BF16).ap()

    items, sched, silu_at, hdn_at, act2_at, tail_items, tail_acts = _ffn_schedule()

    ctx = ExitStack()
    with ctx:
        sb = lambda n, s, d: ctx.enter_context(nc.sbuf_tensor(n, list(s), d))
        sem = lambda n: Sem(ctx, nc, n)
        block = ctx.enter_context(nc.Block())

        xn_T = sb("xn_T", [128, NOC * 2048], BF16)
        g_T = sb("g_T", [128, 16 * 1024], BF16)
        ctx_sb = sb("ctx_sb", [128, 2 * T], BF16)
        KT = sb("KT", [128, 4096], BF16)
        VS = sb("VS", [128, 64 * 65], BF16)
        gsc_sb = sb("gsc_sb", [128, NIT], F32)
        usc_sb = sb("usc_sb", [128, NIT], F32)
        dsc_sb = sb("dsc_sb", [128, 16], F32)
        gsch_sb = sb("gsch_sb", [128, NIT], F32)
        pswap_sb = sb("pswap_sb", [128, 128], BF16)
        ident_sb = sb("ident_sb", [128, 128], BF16)
        ones_sb = sb("ones_sb", [1, 128], F32)
        sink_sb = sb("sink_sb", [1, 4], F32)
        esink_sb = sb("esink_sb", [1, 4], F32)
        konst_sb = sb("konst_sb", [128, 4], F32)

        LDC = sem("LDC")
        CCX = sem("CCX")
        CCA = sem("CCA")

        def xn_rhs(dt, n2):
            return xn_T[:, dt * 1024 + n2 * 512:dt * 1024 + (n2 + 1) * 512]

        @block.gpsimd
        def _(g):
            for dst, src in ((gsc_sb, gsc), (usc_sb, usc), (dsc_sb, dsc),
                             (pswap_sb, pswap), (ident_sb, ident),
                             (ones_sb, onesr), (sink_sb, sink),
                             (konst_sb, konst)):
                g.dma_start(out=dst[:], in_=src).then_inc(LDC.h, 16)
                LDC.n += 16

        with ExitStack() as p0:
            sb0 = lambda n, s, d: p0.enter_context(nc.sbuf_tensor(n, list(s), d))
            ps0 = lambda n, s, d: p0.enter_context(nc.psum_tensor(n, list(s), d))
            xch = [sb0(f"xch{i}", [128, 2048], F32) for i in range(2)]
            junk = sb0("junk0", [128, 2048], F32)
            junk2 = sb0("junk2", [128, 2048], F32)
            stat = sb0("stat", [128, 8], F32)
            xnm = sb0("xnm", [128, 2048], BF16)
            xnchT = [sb0(f"xnchT{i}", [128, 2048], BF16) for i in range(2)]
            xnT_ps = ps0("xnT_ps", [128, 2048], BF16)

            LDX = [sem("LDX0"), sem("LDX1")]
            VE0, PE0, SE0 = sem("VE0"), sem("PE0"), sem("SE0")
            STX = [sem("STX0"), sem("STX1")]

            @block.sync
            def _(sy):
                for ct in range(NOC):
                    if ct >= 2:
                        sy.wait_ge(VE0.h, (ct - 2) * VE_CH0 + 7)
                    sy.dma_start(out=xch[ct % 2][:],
                                 in_=x_c[ct * 128:(ct + 1) * 128, :]).then_inc(LDX[ct % 2].h, 16)
                    LDX[ct % 2].n += 16

            @block.scalar
            def _(s):
                s.wait_ge(LDC.h, LDC.n)
                for ct in range(NOC):
                    s.wait_ge(VE0.h, ct * VE_CH0 + 5)
                    ins = s.activation(stat[:, 7:8], stat[:, 4:5], AF.Ln,
                                       bias=konst_sb[:, 0:1])
                    ins.then_inc(SE0.h, 1); SE0.n += 1
                    s.wait_ge(SE0.h, ct * 2 + 1)
                    ins = s.activation(stat[:, 5:6], stat[:, 7:8], AF.Exp,
                                       scale=konst_sb[:, 1:2])
                    ins.then_inc(SE0.h, 1); SE0.n += 1

            @block.vector
            def _(v):
                def vdo(ins):
                    ins.then_inc(VE0.h, 1); VE0.n += 1
                    return ins
                for ct in range(NOC):
                    xc = xch[ct % 2]
                    v.wait_ge(LDX[ct % 2].h, (ct // 2 + 1) * 16)
                    vdo(v.tensor_scalar(junk[:], xc[:], 1.0, 0.0, op0=ALU.mult,
                                        op1=ALU.add, accum_out=stat[:, 0:1]))
                    vdo(v.scalar_tensor_tensor(junk2[:], xc[:], 1.0, xc[:], op0=ALU.mult,
                                               op1=ALU.mult, accum_out=stat[:, 1:2]))
                    v.wait_ge(VE0.h, VE0.n)
                    vdo(v.tensor_scalar(stat[:, 2:3], stat[:, 0:1], 1.0 / D, None, op0=ALU.mult))
                    v.wait_ge(VE0.h, VE0.n)
                    vdo(v.tensor_tensor(stat[:, 3:4], stat[:, 2:3], stat[:, 2:3], op=ALU.mult))
                    v.wait_ge(VE0.h, VE0.n)
                    vdo(v.scalar_tensor_tensor(stat[:, 4:5], stat[:, 1:2], 1.0 / D,
                                               stat[:, 3:4], op0=ALU.mult, op1=ALU.subtract))
                    v.wait_ge(SE0.h, (ct + 1) * 2)
                    vdo(v.tensor_scalar(stat[:, 6:7], stat[:, 2:3], stat[:, 5:6], -1.0,
                                        op0=ALU.mult, op1=ALU.mult))
                    v.wait_ge(VE0.h, VE0.n)
                    vdo(v.tensor_scalar(xnm[:], xc[:], stat[:, 5:6], stat[:, 6:7],
                                        op0=ALU.mult, op1=ALU.add))
                    v.wait_ge(PE0.h, ct + 1)
                    if ct >= 2:
                        v.wait_ge(STX[ct % 2].h, (ct // 2) * 16)
                    vdo(v.tensor_copy(xnchT[ct % 2][:], xnT_ps[:]))
                    vdo(v.tensor_copy(
                        xn_T[:].rearrange("p (dt ct2 t) -> p dt ct2 t",
                                          dt=NDT, ct2=NOC)[:, :, ct, :],
                        xnT_ps[:].rearrange("p (dt t) -> p dt t", dt=NDT)))

            @block.tensor
            def _(t):
                t.wait_ge(LDC.h, LDC.n)
                for ct in range(NOC):
                    t.wait_ge(VE0.h, ct * VE_CH0 + 7)
                    if ct >= 1:
                        t.wait_ge(VE0.h, (ct - 1) * VE_CH0 + 9)
                    for dt in range(NDT):
                        ins = t.transpose(xnT_ps[:, dt * 128:(dt + 1) * 128],
                                          xnm[:, dt * 128:(dt + 1) * 128], ident_sb[:])
                    ins.then_inc(PE0.h, 1); PE0.n += 1

            sxr = stage_xn.rearrange("(dt ct p) t -> ct p dt t", dt=NDT, ct=NOC)

            @block.gpsimd
            def _(g):
                for ct in range(NOC):
                    g.wait_ge(VE0.h, ct * VE_CH0 + 8)
                    g.dma_start(out=sxr[ct],
                                in_=xnchT[ct % 2][:].rearrange(
                                    "p (dt t) -> p dt t", dt=NDT)).then_inc(STX[ct % 2].h, 16)
                    STX[ct % 2].n += 16
                g.wait_ge(STX[0].h, STX[0].n)
                g.wait_ge(STX[1].h, STX[1].n)

            p0_final = dict(VE0=(VE0.h, VE0.n), PE0=(PE0.h, PE0.n),
                            SE0=(SE0.h, SE0.n), STX0=(STX[0].h, STX[0].n),
                            STX1=(STX[1].h, STX[1].n))

        with ExitStack() as pk:
            sbk = lambda n, s, d: pk.enter_context(nc.sbuf_tensor(n, list(s), d))
            psk = lambda n, s, d: pk.enter_context(nc.psum_tensor(n, list(s), d))
            xnW = [sbk(f"xnW{i}", [128, NDT * 512], BF16) for i in range(2)]
            wkv_sb = sbk("wkv_sb", [128, 16 * 128], BF16)
            wq_sb = sbk("wq_sb", [128, 16 * 256], BF16)
            cosQ = [sbk(f"cosQ{i}", [128, 512], F32) for i in range(2)]
            sinQ = [sbk(f"sinQ{i}", [128, 512], F32) for i in range(2)]
            kbf = sbk("kbf", [64, 512], BF16)
            vbf = sbk("vbf", [64, 512], BF16)
            t1s = sbk("t1s", [64, 512], F32)
            t2s = sbk("t2s", [64, 512], F32)
            qbf = sbk("qbf", [128, 512], BF16)
            t1q = sbk("t1q", [128, 512], F32)
            t2q = sbk("t2q", [128, 512], F32)
            qstage = [sbk(f"qstage{i}", [128, 512], BF16) for i in range(2)]
            kvps = psk("kvps", [128, 512], F32)
            ksw = psk("ksw", [64, 512], F32)
            vT_ps = psk("vT_ps", [128, 256], BF16)
            qp = [psk(f"qp{i}", [128, 512], F32) for i in range(2)]
            qsw = psk("qsw", [128, 512], F32)

            LXW = [sem("LXWa"), sem("LXWb")]
            LWV, LRT = sem("LWV"), sem("LRT")
            PEK, VEK = sem("PEK"), sem("VEK")
            STQ = [sem("STQa"), sem("STQb")]

            xar = xn_all.rearrange("(rk dt ct p) t -> rk dt ct p t",
                                   rk=NCORES, dt=NDT, ct=NOC)

            @block.gpsimd
            def _(g):
                for hs, vv in p0_final.items():
                    g.wait_ge(vv[0], vv[1])
                g.dma_start(out=wkv_sb[:].rearrange("p (dt c) -> p dt c", dt=16),
                            in_=wkv_r.rearrange("dt p c -> p dt c")).then_inc(LWV.h, 16)
                g.dma_start(out=wq_sb[:].rearrange("p (dt c) -> p dt c", dt=16),
                            in_=wq_r.rearrange("dt p c -> p dt c")).then_inc(LWV.h, 16)
                LWV.n += 32
                g.collective_compute(
                    "AllGather", ALU.bypass,
                    replica_groups=[list(range(NCORES))],
                    ins=[stage_xn.tensor.ap().opt()],
                    outs=[xn_all.tensor.ap().opt()],
                ).then_inc(CCX.h)
                CCX.n = 1

            @block.sync
            def _(sy):
                for hs, vv in p0_final.items():
                    sy.wait_ge(vv[0], vv[1])
                sy.wait_ge(CCX.h, 1)
                for tb in range(NTB):
                    rk, half = tb // 2, tb % 2
                    if tb >= 2:
                        sy.wait_ge(PEK.h, (tb - 1) * PE_CHK)
                    for dt in range(NDT):
                        src = xar[rk, dt, 4 * half:4 * half + 4, :, :]
                        sy.dma_start(
                            out=xnW[tb % 2][:, dt * 512:(dt + 1) * 512],
                            in_=src.rearrange("ct p t -> p ct t"),
                        ).then_inc(LXW[tb % 2].h, 16)
                        LXW[tb % 2].n += 16
                    if tb >= 1:
                        sy.wait_ge(VEK.h, (tb - 1) * VE_CHK + 1)
                    sy.dma_start(out=cosQ[tb % 2][:],
                                 in_=cosg[:, tb * 512:(tb + 1) * 512]).then_inc(LRT.h, 16)
                    sy.dma_start(out=sinQ[tb % 2][:],
                                 in_=sing[:, tb * 512:(tb + 1) * 512]).then_inc(LRT.h, 16)
                    LRT.n += 32
                    if tb >= 1:
                        for mp in range(2):
                            sy.wait_ge(VEK.h, (tb - 1) * VE_CHK + 11 + 4 * mp)
                            sy.dma_start(out=q_d[:, mp * T + (tb - 1) * 512:mp * T + tb * 512],
                                         in_=qstage[mp][:]).then_inc(STQ[mp].h, 16)
                            STQ[mp].n += 16
                for mp in range(2):
                    sy.wait_ge(VEK.h, (NTB - 1) * VE_CHK + 11 + 4 * mp)
                    sy.dma_start(out=q_d[:, mp * T + (NTB - 1) * 512:mp * T + NTB * 512],
                                 in_=qstage[mp][:]).then_inc(STQ[mp].h, 16)
                    STQ[mp].n += 16

            @block.tensor
            def _(t):
                for hs, vv in p0_final.items():
                    t.wait_ge(vv[0], vv[1])
                t.wait_ge(LWV.h, 32)
                for tb in range(NTB):
                    t.wait_ge(LXW[tb % 2].h, (tb // 2 + 1) * 256)
                    xw = xnW[tb % 2]
                    if tb >= 1:
                        t.wait_ge(VEK.h, (tb - 1) * VE_CHK + 4)
                    for dt in range(NDT):
                        ins = t.matmul(kvps[:],
                                       wkv_sb[:, dt * 128:(dt + 1) * 128],
                                       xw[:, dt * 512:(dt + 1) * 512],
                                       start=(dt == 0), stop=(dt == NDT - 1))
                    ins.then_inc(PEK.h, 1); PEK.n += 1
                    t.wait_ge(VEK.h, tb * VE_CHK + 1)
                    ins = t.matmul(ksw[:], pswap_sb[0:64, 0:64], kbf[:],
                                   start=True, stop=True)
                    ins.then_inc(PEK.h, 1); PEK.n += 1
                    t.wait_ge(VEK.h, tb * VE_CHK + 2)
                    if tb >= 1:
                        t.wait_ge(VEK.h, (tb - 1) * VE_CHK + 7)
                    for gi in range(4):
                        ins = t.transpose(vT_ps[:, gi * 64:(gi + 1) * 64],
                                          vbf[:, gi * 128:(gi + 1) * 128],
                                          ident_sb[0:64, 0:64])
                    ins.then_inc(PEK.h, 1); PEK.n += 1
                    for mp in range(2):
                        if tb >= 1:
                            t.wait_ge(VEK.h, (tb - 1) * VE_CHK + 10 + 4 * mp)
                        for dt in range(NDT):
                            ins = t.matmul(qp[mp][:],
                                           wq_sb[:, dt * 256 + mp * 128:dt * 256 + (mp + 1) * 128],
                                           xw[:, dt * 512:(dt + 1) * 512],
                                           start=(dt == 0), stop=(dt == NDT - 1))
                        ins.then_inc(PEK.h, 1); PEK.n += 1
                        t.wait_ge(VEK.h, tb * VE_CHK + 8 + mp * 4)
                        if tb >= 1 and mp == 0:
                            t.wait_ge(VEK.h, (tb - 1) * VE_CHK + 13)
                        if mp == 1:
                            t.wait_ge(VEK.h, tb * VE_CHK + 9)
                        ins = t.matmul(qsw[:], pswap_sb[:], qbf[:],
                                       start=True, stop=True)
                        ins.then_inc(PEK.h, 1); PEK.n += 1

            @block.vector
            def _(v):
                def vdo(ins):
                    ins.then_inc(VEK.h, 1); VEK.n += 1
                    return ins
                v.memset(VS[:].rearrange("k (g c) -> k g c", c=65)[:, :, 64:65], 1.0)
                for hs, vv in p0_final.items():
                    v.wait_ge(vv[0], vv[1])
                for tb in range(NTB):
                    v.wait_ge(LRT.h, (tb + 1) * 32)
                    v.wait_ge(PEK.h, tb * PE_CHK + 1)
                    vdo(v.tensor_copy(kbf[:], kvps[0:64, :]))
                    vdo(v.tensor_copy(vbf[:], kvps[64:128, :]))
                    v.wait_ge(PEK.h, tb * PE_CHK + 2)
                    vdo(v.tensor_tensor(t1s[:], ksw[:], sinQ[tb % 2][0:64, :], op=ALU.mult))
                    vdo(v.tensor_tensor(t2s[:], kvps[0:64, :], cosQ[tb % 2][0:64, :], op=ALU.mult))
                    v.wait_ge(VEK.h, VEK.n)
                    t1v = t1s[:].rearrange("p (gi2 two k) -> p two gi2 k", two=2, k=128)
                    t2v = t2s[:].rearrange("p (gi2 two k) -> p two gi2 k", two=2, k=128)
                    ktv = KT[:].rearrange("p (b k) -> p b k", k=128)
                    vdo(v.tensor_tensor(ktv[0:64, 2 * tb:2 * tb + 2, :],
                                        t1v[:, 0], t2v[:, 0], op=ALU.add))
                    vdo(v.tensor_tensor(ktv[64:128, 2 * tb:2 * tb + 2, :],
                                        t1v[:, 1], t2v[:, 1], op=ALU.add))
                    v.wait_ge(PEK.h, tb * PE_CHK + 3)
                    vdo(v.tensor_copy(
                        VS[:].rearrange("k (g c) -> k g c", c=65)
                        [:, 4 * tb:4 * tb + 4, 0:64],
                        vT_ps[:].rearrange("k (g c) -> k g c", c=64)))
                    for mp in range(2):
                        v.wait_ge(PEK.h, tb * PE_CHK + 4 + mp * 2)
                        vdo(v.tensor_copy(qbf[:], qp[mp][:]))
                        v.wait_ge(PEK.h, tb * PE_CHK + 5 + mp * 2)
                        vdo(v.tensor_tensor(t1q[:], qsw[:], sinQ[tb % 2][:], op=ALU.mult))
                        vdo(v.tensor_tensor(t2q[:], qp[mp][:], cosQ[tb % 2][:], op=ALU.mult))
                        v.wait_ge(VEK.h, VEK.n)
                        if tb >= 1:
                            v.wait_ge(STQ[mp].h, tb * 16)
                        vdo(v.tensor_tensor(qstage[mp][:], t1q[:], t2q[:], op=ALU.add))

            pkv_final = dict(PEK=(PEK.h, PEK.n), VEK=(VEK.h, VEK.n),
                             STQa=(STQ[0].h, STQ[0].n), STQb=(STQ[1].h, STQ[1].n),
                             LXWa=(LXW[0].h, LXW[0].n), LXWb=(LXW[1].h, LXW[1].n))

        with ExitStack() as b2c:
            sbc = lambda n, s, d: b2c.enter_context(nc.sbuf_tensor(n, list(s), d))
            psc = lambda n, s, d: b2c.enter_context(nc.psum_tensor(n, list(s), d))
            masks_sb = sbc("masks_sb", [128, 2432], BF16)
            QS = [sbc(f"QS{i}", [128, 1024], BF16) for i in range(2)]
            ES = [sbc(f"ES{i}", [128, 1024], BF16) for i in range(3)]
            dsum = sbc("dsum", [1, 1024], F32)
            rbc = sbc("rbc", [128, 1024], F32)
            scps = [psc(f"scps{i}", [128, 1024], F32) for i in range(2)]
            ctxp = psc("ctxp", [65, 1024], F32)

            LDM = sem("LDM")
            LQS = [sem("LQSa"), sem("LQSb")]
            QKs, EXs, MKs, WVs, BCs, VE4, SE4 = (sem("QKs"), sem("EXs"), sem("MKs"),
                                                 sem("WVs"), sem("BCs"), sem("VE4"),
                                                 sem("SE4"))
            LD1 = [sem("LD1a"), sem("LD1b")]
            STH = [sem("STHa"), sem("STHb")]
            PE1, VE1, SE1 = sem("PE1"), sem("VE1"), sem("SE1")
            LD2 = [sem("LD2a"), sem("LD2b")]
            LDH = sem("LDH")
            PE2, SE2 = sem("PE2"), sem("SE2")
            GSH = sem("GSH")
            gux_at = {gk + 1: lst for gk, lst in silu_at.items()}

            mk_after = {}
            _cnt = 0
            for _ob in range(NOB):
                _qb = _ob % NQB
                for _sp in range(32):
                    for _par in range(2):
                        _j = _ob * 64 + _sp * 2 + _par
                        if masked_pi(_qb, _sp) is not None:
                            _cnt += 1
                        mk_after[_j] = _cnt

            fstate = {}

            def emit_item(t, idx):
                kind, it, k, av = items[idx]
                if kind == "g":
                    if k == 0:
                        t.wait_ge(LD1[it % 2].h, (it // 2 + 1) * 32)
                        if it >= 1:
                            t.wait_ge(VE1.h, 3 * it)
                    dt, n2 = k // 2, k % 2
                    ins = t.matmul(fstate["gu_ps"][:, n2 * 512:(n2 + 1) * 512],
                                   fstate["gw"][it % 2][:, dt * 128:(dt + 1) * 128],
                                   xn_rhs(dt, n2),
                                   start=(dt == 0), stop=(dt == NDT - 1))
                    if k == 31:
                        ins.then_inc(PE1.h, 1); PE1.n += 1
                elif kind == "u":
                    if k == 0:
                        t.wait_ge(SE1.h, it + 1)
                        t.wait_ge(VE1.h, 3 * it + 1)
                    dt, n2 = k // 2, k % 2
                    ins = t.matmul(fstate["gu_ps"][:, n2 * 512:(n2 + 1) * 512],
                                   fstate["uw"][it % 2][:, dt * 128:(dt + 1) * 128],
                                   xn_rhs(dt, n2),
                                   start=(dt == 0), stop=(dt == NDT - 1))
                    if k == 31:
                        ins.then_inc(PE1.h, 1); PE1.n += 1
                else:
                    it2 = it
                    if k == 0:
                        th = it2 // 16
                        t.wait_ge(LD2[it2 % 2].h, (it2 // 2 + 1) * 16)
                        t.wait_ge(LDH.h, (th + 1) * 64)
                        if it2 >= 2:
                            t.wait_ge(SE2.h, it2 - 1)
                    ins = t.matmul(fstate["d_ps"][it2 % 2][:],
                                   fstate["dw"][it2 % 2][:, k * 128:(k + 1) * 128],
                                   fstate["hdn_half"][:, k * 512:(k + 1) * 512],
                                   start=(k == 0), stop=(k == NIT - 1))
                    if k == NIT - 1:
                        ins.then_inc(PE2.h, 1); PE2.n += 1

            def emit_wv(t, ob, sp, par):
                j = ob * 64 + sp * 2 + par
                qb = ob % NQB
                if masked_pi(qb, sp) is not None:
                    t.wait_ge(MKs.h, mk_after[j])
                else:
                    t.wait_ge(EXs.h, j + 1)
                if sp == 0 and par == 0:
                    t.wait_ge(VE4.h, ob * 4)
                ls = sp * 2 + par
                for qh in range(2):
                    ins = t.matmul(ctxp[:, qh * 512:(qh + 1) * 512],
                                   VS[:, ls * 65:ls * 65 + 65],
                                   ES[j % 3][:, qh * 512:(qh + 1) * 512],
                                   start=(sp == 0 and par == 0),
                                   stop=(sp == 31 and par == 1))
                ins.then_inc(WVs.h, 1); WVs.n += 1

            def emit_ob_tensor(t, ob):
                t.wait_ge(LQS[ob % 2].h, (ob // 2 + 1) * 32)
                for sp in range(32):
                    gs = ob * 32 + sp
                    j0 = ob * 64 + sp * 2
                    if j0 >= 2:
                        t.wait_ge(EXs.h, j0 - 1)
                    if sp == 0 and ob >= 1:
                        t.wait_ge(VE4.h, (ob - 1) * 4 + 3)
                    for qh in range(2):
                        ins = t.matmul(scps[0][:, qh * 512:(qh + 1) * 512],
                                       KT[0:64, sp * 128:(sp + 1) * 128],
                                       QS[ob % 2][0:64, qh * 512:(qh + 1) * 512],
                                       start=True, stop=True, tile_position=(0, 0))
                    ins.then_inc(QKs.h, 1); QKs.n += 1
                    if j0 >= 1:
                        t.wait_ge(EXs.h, j0)
                    for qh in range(2):
                        ins = t.matmul(scps[1][:, qh * 512:(qh + 1) * 512],
                                       KT[64:128, sp * 128:(sp + 1) * 128],
                                       QS[ob % 2][64:128, qh * 512:(qh + 1) * 512],
                                       start=True, stop=True, tile_position=(64, 0))
                    ins.then_inc(QKs.h, 1); QKs.n += 1
                    for idx in sched[gs]:
                        emit_item(t, idx)
                    if sp >= 1:
                        emit_wv(t, ob, sp - 1, 0)
                        emit_wv(t, ob, sp - 1, 1)
                emit_wv(t, ob, 31, 0)
                emit_wv(t, ob, 31, 1)
                t.wait_ge(VE4.h, ob * 4 + 2)
                t.wait_ge(EXs.h, ob * 64 + 63)
                for qh in range(2):
                    ins = t.matmul(scps[0][:, qh * 512:(qh + 1) * 512],
                                   ones_sb[:],
                                   dsum[0:1, qh * 512:(qh + 1) * 512],
                                   start=True, stop=True)
                ins.then_inc(BCs.h, 1); BCs.n += 1

            def emit_ob_scalar(s, ob):
                for sp in range(32):
                    gs = ob * 32 + sp
                    for par in range(2):
                        j = ob * 64 + sp * 2 + par
                        s.wait_ge(QKs.h, j + 1)
                        if j >= 3:
                            s.wait_ge(WVs.h, j - 2)
                        ins = s.activation(ES[j % 3][:], scps[par][:],
                                           AF.Exp, scale=konst_sb[:, 2:3])
                        ins.then_inc(EXs.h, 1); EXs.n += 1
                    for it in silu_at.get(gs, ()):
                        s.wait_ge(PE1.h, it * 2 + 1)
                        if it >= 1:
                            s.wait_ge(VE1.h, 3 * it)
                        ins = s.activation(fstate["sil"][:], fstate["gu_ps"][:],
                                           AF.Tanh, scale=gsch_sb[:, it:it + 1])
                        ins.then_inc(SE1.h, 1); SE1.n += 1
                    for it2 in act2_at.get(gs, ()):
                        th, dmt = it2 // 16, it2 % 16
                        s.wait_ge(PE2.h, it2 + 1)
                        ins = s.activation(
                            g_T[:, dmt * 1024 + th * 512:dmt * 1024 + th * 512 + 512],
                            fstate["d_ps"][it2 % 2][:], AF.Silu,
                            scale=dsc_sb[:, dmt:dmt + 1])
                        ins.then_inc(SE2.h, 1); SE2.n += 1

            def emit_ob_vector(v, ob):
                m, qb = ob // NQB, ob % NQB
                for sp in range(32):
                    gs = ob * 32 + sp
                    for it in gux_at.get(gs, ()):
                        v.wait_ge(PE1.h, it * 2 + 1)
                        v.wait_ge(VE1.h, VE1.n)
                        ins = v.tensor_scalar(fstate["gprod"][:], fstate["gu_ps"][:],
                                              gsch_sb[:, it:it + 1], 0.0,
                                              op0=ALU.mult, op1=ALU.add)
                        ins.then_inc(VE1.h, 1); VE1.n += 1
                        v.wait_ge(SE1.h, it + 1)
                        v.wait_ge(VE1.h, VE1.n)
                        ins = v.scalar_tensor_tensor(fstate["sil"][:], fstate["sil"][:],
                                                     1.0, fstate["gprod"][:],
                                                     op0=ALU.add, op1=ALU.mult)
                        ins.then_inc(VE1.h, 1); VE1.n += 1
                    for it in hdn_at.get(gs, ()):
                        v.wait_ge(PE1.h, it * 2 + 2)
                        v.wait_ge(VE1.h, 3 * it + 2)
                        if it >= 2:
                            v.wait_ge(STH[it % 2].h, (it // 2) * 16)
                        ins = v.scalar_tensor_tensor(
                            fstate["hdnst"][it % 2][:], fstate["gu_ps"][:],
                            usc_sb[:, it:it + 1], fstate["sil"][:],
                            op0=ALU.mult, op1=ALU.mult)
                        ins.then_inc(VE1.h, 1); VE1.n += 1
                    pi = masked_pi(qb, sp)
                    if pi is not None:
                        for par in range(2):
                            j = ob * 64 + sp * 2 + par
                            dlt = 2 * (sp - 4 * qb) + par
                            v.wait_ge(EXs.h, j + 1)
                            ins = v.tensor_tensor(
                                ES[j % 3][:], ES[j % 3][:],
                                masks_sb[:, 896 - 128 * dlt:896 - 128 * dlt + 1024],
                                op=ALU.mult)
                            ins.then_inc(MKs.h, 1); MKs.n += 1
                def vdo4(ins):
                    ins.then_inc(VE4.h, 1); VE4.n += 1
                    return ins
                v.wait_ge(WVs.h, (ob + 1) * 64)
                vdo4(v.tensor_copy(dsum[:], ctxp[64:65, :]))
                v.wait_ge(VE4.h, VE4.n)
                vdo4(v.tensor_scalar(dsum[:], dsum[:],
                                     esink_sb[0:1, m:m + 1],
                                     None, op0=ALU.add))
                v.wait_ge(BCs.h, ob + 1)
                vdo4(v.reciprocal(rbc[:], scps[0][:]))
                v.wait_ge(VE4.h, VE4.n)
                vdo4(v.tensor_tensor(
                    ctx_sb[(m % 2) * 64:(m % 2) * 64 + 64,
                           (m // 2) * T + qb * 1024:(m // 2) * T + (qb + 1) * 1024],
                    ctxp[0:64, :], rbc[0:64, :], op=ALU.mult))

            def emit_ob_sync(sy, ob):
                m, qb = ob // NQB, ob % NQB
                if ob >= 2:
                    sy.wait_ge(QKs.h, (ob - 1) * 64)
                half = q_d[(m % 2) * 64:(m % 2) * 64 + 64,
                           (m // 2) * T + qb * 1024:(m // 2) * T + (qb + 1) * 1024]
                sy.dma_start(out=QS[ob % 2][0:64, :], in_=half).then_inc(LQS[ob % 2].h, 16)
                sy.dma_start(out=QS[ob % 2][64:128, :], in_=half).then_inc(LQS[ob % 2].h, 16)
                LQS[ob % 2].n += 32

            with ExitStack() as b2a:
                sba = lambda n, s, d: b2a.enter_context(nc.sbuf_tensor(n, list(s), d))
                fstate["gw"] = [sba(f"gw{i}", [128, 2048], BF16) for i in range(2)]
                fstate["uw"] = [sba(f"uw{i}", [128, 2048], BF16) for i in range(2)]
                fstate["sil"] = sba("sil", [128, 1024], F32)
                fstate["gprod"] = sba("gprod", [128, 1024], F32)
                fstate["hdnst"] = [sba(f"hdnst{i}", [128, 1024], BF16) for i in range(2)]
                fstate["gu_ps"] = b2a.enter_context(nc.psum_tensor("gu_ps", [128, 1024], F32))

                @block.gpsimd
                def _(g):
                    for hs, vv in pkv_final.items():
                        g.wait_ge(vv[0], vv[1])
                    g.dma_start(out=masks_sb[:], in_=masks).then_inc(LDM.h, 16)
                    LDM.n += 16
                    for it in range(NIT):
                        if it >= 2:
                            g.wait_ge(PE1.h, (it - 1) * 2)
                        g.dma_start(out=fstate["gw"][it % 2][:].rearrange(
                            "p (dt c) -> p dt c", dt=16),
                            in_=gw_r[it]).then_inc(LD1[it % 2].h, 16)
                        LD1[it % 2].n += 16
                        g.dma_start(out=fstate["uw"][it % 2][:].rearrange(
                            "p (dt c) -> p dt c", dt=16),
                            in_=uw_r[it]).then_inc(LD1[it % 2].h, 16)
                        LD1[it % 2].n += 16
                        if it >= 2:
                            g.wait_ge(VE1.h, 3 * (it - 2) + 3)
                            g.dma_start(out=hdn_d[it - 2],
                                        in_=fstate["hdnst"][it % 2][:]).then_inc(STH[it % 2].h, 16)
                            STH[it % 2].n += 16
                    for it in (NIT - 2, NIT - 1):
                        g.wait_ge(VE1.h, 3 * it + 3)
                        g.dma_start(out=hdn_d[it],
                                    in_=fstate["hdnst"][it % 2][:]).then_inc(STH[it % 2].h, 16)
                        STH[it % 2].n += 16

                @block.scalar
                def _(s):
                    s.wait_ge(LDC.h, LDC.n)
                    s.wait_ge(GSH.h, 1)
                    ins = s.activation(esink_sb[:], sink_sb[:], AF.Exp)
                    ins.then_inc(SE4.h, 1); SE4.n += 1
                    for ob in range(OB_SPLIT):
                        emit_ob_scalar(s, ob)

                @block.sync
                def _(sy):
                    for hs, vv in pkv_final.items():
                        sy.wait_ge(vv[0], vv[1])
                    for ob in range(OB_SPLIT):
                        emit_ob_sync(sy, ob)

                @block.tensor
                def _(t):
                    for hs, vv in pkv_final.items():
                        t.wait_ge(vv[0], vv[1])
                    for ob in range(OB_SPLIT):
                        emit_ob_tensor(t, ob)

                @block.vector
                def _(v):
                    v.wait_ge(LDC.h, LDC.n)
                    ins = v.tensor_scalar(gsch_sb[:], gsc_sb[:], 0.5, None, op0=ALU.mult)
                    ins.then_inc(GSH.h, 1)
                    v.wait_ge(GSH.h, 1)
                    v.wait_ge(SE4.h, 1)
                    v.wait_ge(LDM.h, 16)
                    for ob in range(OB_SPLIT):
                        emit_ob_vector(v, ob)

            with ExitStack() as b2b:
                sbb = lambda n, s, d: b2b.enter_context(nc.sbuf_tensor(n, list(s), d))
                fstate["dw"] = [sbb(f"dw{i}", [128, NIT * 128], BF16) for i in range(2)]
                fstate["hdn_half"] = sbb("hdn_half", [128, NIT * 512], BF16)
                fstate["d_ps"] = [b2b.enter_context(
                    nc.psum_tensor(f"d_ps{i}", [128, 512], F32)) for i in range(2)]
                hdn_r = hdn_d.rearrange("it p t -> p it t")

                @block.gpsimd
                def _(g):
                    for th in range(2):
                        if th == 0:
                            g.wait_ge(STH[0].h, STH[0].n)
                            g.wait_ge(STH[1].h, STH[1].n)
                        else:
                            g.wait_ge(PE2.h, 16)
                        src = hdn_r[:, :, th * 512:(th + 1) * 512]
                        for q in range(4):
                            g.dma_start(out=fstate["hdn_half"]
                                        [:, q * 11 * 512:(q + 1) * 11 * 512]
                                        .rearrange("p (it t) -> p it t", it=11),
                                        in_=src[:, q * 11:(q + 1) * 11, :]).then_inc(LDH.h, 16)
                            LDH.n += 16
                        for it2 in range(th * 16, th * 16 + 16):
                            dmt = it2 % 16
                            if it2 >= 2:
                                g.wait_ge(PE2.h, it2 - 1)
                            g.dma_start(out=fstate["dw"][it2 % 2][:].rearrange(
                                "p (it c) -> p it c", it=NIT),
                                in_=dw_r[dmt]).then_inc(LD2[it2 % 2].h, 16)
                            LD2[it2 % 2].n += 16

                @block.scalar
                def _(s):
                    for ob in range(OB_SPLIT, NOB):
                        emit_ob_scalar(s, ob)
                    late = [it2 for gs in range(GS_N, GS_N + 64)
                            for it2 in act2_at.get(gs, ())] + tail_acts
                    for it2 in late:
                        th, dmt = it2 // 16, it2 % 16
                        s.wait_ge(PE2.h, it2 + 1)
                        ins = s.activation(
                            g_T[:, dmt * 1024 + th * 512:dmt * 1024 + th * 512 + 512],
                            fstate["d_ps"][it2 % 2][:], AF.Silu,
                            scale=dsc_sb[:, dmt:dmt + 1])
                        ins.then_inc(SE2.h, 1); SE2.n += 1

                @block.sync
                def _(sy):
                    for ob in range(OB_SPLIT, NOB):
                        emit_ob_sync(sy, ob)

                @block.tensor
                def _(t):
                    for ob in range(OB_SPLIT, NOB):
                        emit_ob_tensor(t, ob)
                    for idx in tail_items:
                        emit_item(t, idx)

                @block.vector
                def _(v):
                    for ob in range(OB_SPLIT, NOB):
                        emit_ob_vector(v, ob)

                down_final = dict(PE2=(PE2.h, PE2.n), SE2=(SE2.h, SE2.n))

            p2_final = dict(QKs=(QKs.h, QKs.n), EXs=(EXs.h, EXs.n),
                            WVs=(WVs.h, WVs.n), VE4=(VE4.h, VE4.n),
                            BCs=(BCs.h, BCs.n), MKs=(MKs.h, MKs.n))

        STC = sem("STC")

        @block.gpsimd
        def _(g):
            for hs, vv in p2_final.items():
                g.wait_ge(vv[0], vv[1])
            csr = ctx_stage.rearrange("(j rg p) t -> rg p j t", j=NCORES, rg=2)
            cbr = ctx_sb[:].rearrange("p (rg j t) -> p rg j t", rg=2, j=NCORES)
            for rg in range(2):
                g.dma_start(out=csr[rg], in_=cbr[:, rg]).then_inc(STC.h, 16)
                STC.n += 16
            g.wait_ge(STC.h, STC.n)
            with allow_ici_all_to_all():
                g.collective_compute(
                    "AllToAll", ALU.bypass,
                    replica_groups=[list(range(NCORES))],
                    ins=[ctx_stage.tensor.ap().opt()],
                    outs=[ctx_all.tensor.ap().opt()],
                ).then_inc(CCA.h)
            CCA.n = 1

        with ExitStack() as p3:
            sb4 = lambda n, s, d: p3.enter_context(nc.sbuf_tensor(n, list(s), d))
            ps4 = lambda n, s, d: p3.enter_context(nc.psum_tensor(n, list(s), d))
            ctxR = sb4("ctxR", [128, 16 * 1024], BF16)
            wo_sb = [sb4(f"wo{i}", [128, 2048], BF16) for i in range(2)]
            xt_sb = [sb4(f"xt{i}", [128, 1024], F32) for i in range(2)]
            prod = sb4("prod", [128, 1024], F32)
            outst = [sb4(f"outst{i}", [128, 1024], F32) for i in range(2)]
            s_ps = [ps4(f"s_ps{i}", [128, 1024], F32) for i in range(2)]

            LD5 = [sem("LD5a"), sem("LD5b")]
            LCX = sem("LCX")
            LX3 = [sem("LX3a"), sem("LX3b")]
            STO = [sem("STOa"), sem("STOb")]
            PE5, VE5 = sem("PE5"), sem("VE5")

            @block.gpsimd
            def _(g):
                for hs, vv in down_final.items():
                    g.wait_ge(vv[0], vv[1])
                g.wait_ge(CCA.h, 1)
                for dv in range(16):
                    g.dma_start(out=ctxR[:, dv * 1024:(dv + 1) * 1024],
                                in_=ctx_all[dv * 128:(dv + 1) * 128, :]).then_inc(LCX.h, 16)
                    LCX.n += 16
                for dmt in range(16):
                    if dmt >= 2:
                        g.wait_ge(PE5.h, dmt - 1)
                    g.dma_start(out=wo_sb[dmt % 2][:].rearrange("p (dv c) -> p dv c", dv=16),
                                in_=wout_r[dmt]).then_inc(LD5[dmt % 2].h, 16)
                    LD5[dmt % 2].n += 16

            def p3_store(sy, dmt):
                sy.wait_ge(VE5.h, dmt * 2 + 2)
                sy.dma_start(out=out_d[dmt * 128:(dmt + 1) * 128, :],
                             in_=outst[dmt % 2][:]).then_inc(STO[dmt % 2].h, 16)
                STO[dmt % 2].n += 16

            @block.sync
            def _(sy):
                for hs, vv in p2_final.items():
                    sy.wait_ge(vv[0], vv[1])
                for hs, vv in down_final.items():
                    sy.wait_ge(vv[0], vv[1])
                for dmt in range(16):
                    if dmt >= 2:
                        sy.wait_ge(VE5.h, (dmt - 1) * 2)
                    sy.dma_start(out=xt_sb[dmt % 2][:],
                                 in_=xT_c[dmt * 128:(dmt + 1) * 128, :]).then_inc(LX3[dmt % 2].h, 16)
                    LX3[dmt % 2].n += 16
                    if dmt >= 1:
                        p3_store(sy, dmt - 1)
                p3_store(sy, 15)
                sy.wait_ge(STO[0].h, 128)
                sy.wait_ge(STO[1].h, 128)

            @block.tensor
            def _(t):
                t.wait_ge(LCX.h, 256)
                for dmt in range(16):
                    t.wait_ge(LD5[dmt % 2].h, (dmt // 2 + 1) * 16)
                    if dmt >= 2:
                        t.wait_ge(VE5.h, (dmt - 1) * 2)
                    for dv in range(16):
                        for n2 in range(2):
                            ins = t.matmul(s_ps[dmt % 2][:, n2 * 512:(n2 + 1) * 512],
                                           wo_sb[dmt % 2][:, dv * 128:(dv + 1) * 128],
                                           ctxR[:, dv * 1024 + n2 * 512:dv * 1024 + (n2 + 1) * 512],
                                           start=(dv == 0), stop=(dv == 15))
                    ins.then_inc(PE5.h, 1); PE5.n += 1

            @block.vector
            def _(v):
                for hs, vv in down_final.items():
                    v.wait_ge(vv[0], vv[1])
                for dmt in range(16):
                    v.wait_ge(PE5.h, dmt + 1)
                    v.wait_ge(VE5.h, VE5.n)
                    ins = v.tensor_tensor(prod[:], s_ps[dmt % 2][:],
                                          g_T[:, dmt * 1024:(dmt + 1) * 1024], op=ALU.mult)
                    ins.then_inc(VE5.h, 1); VE5.n += 1
                    v.wait_ge(VE5.h, VE5.n)
                    v.wait_ge(LX3[dmt % 2].h, (dmt // 2 + 1) * 16)
                    if dmt >= 2:
                        v.wait_ge(STO[dmt % 2].h, (dmt // 2) * 16)
                    ins = v.tensor_tensor(outst[dmt % 2][:], prod[:], xt_sb[dmt % 2][:],
                                          op=ALU.add)
                    ins.then_inc(VE5.h, 1); VE5.n += 1

    return nc


def _rope_tables():
    import os, subprocess, tempfile
    prog = (
        "import numpy as np, jax.numpy as jnp, sys\n"
        "inv = (1.0 / 1024.0) ** jnp.linspace(0.0, 1.0, %d)\n"
        "inv = jnp.concatenate([inv, jnp.zeros(%d)])\n"
        "th = jnp.arange(%d, dtype=jnp.float32)[:, None] * inv[None, :]\n"
        "np.save(sys.argv[1] + 'c.npy', np.asarray(jnp.cos(th), np.float32))\n"
        "np.save(sys.argv[1] + 's.npy', np.asarray(jnp.sin(th), np.float32))\n"
        % (HD // 4, HD // 4, T))
    try:
        with tempfile.NamedTemporaryFile(suffix="_rope", delete=False) as f:
            path = f.name
        env = dict(os.environ)
        env.pop("TRN_TERMINAL_POOL_IPS", None)
        env["JAX_PLATFORMS"] = "cpu"
        npp = env.get("NIX_PYTHONPATH", "")
        if npp:
            env["PYTHONPATH"] = npp + os.pathsep + env.get("PYTHONPATH", "")
        r = subprocess.run([sys.executable, "-c", prog, path], env=env,
                           capture_output=True, timeout=600)
        if r.returncode == 0:
            return (np.load(path + "c.npy"), np.load(path + "s.npy"))
    except Exception:
        pass
    lin = np.linspace(0.0, 1.0, HD // 4).astype(np.float32)
    inv = np.concatenate([np.float32(1.0 / 1024.0) ** lin,
                          np.zeros(HD // 4, np.float32)])
    th = np.arange(T, dtype=np.float32)[:, None] * inv[None, :]
    return np.cos(th.astype(np.float64)).astype(np.float32), \
        np.sin(th.astype(np.float64)).astype(np.float32)


def _host_prep(L0_attn_x, L0_attn_W_qkv, L0_attn_W_out, L0_attn_s,
               L0_ffn_gate_w, L0_ffn_up_w, L0_ffn_down_w,
               L0_ffn_gate_s, L0_ffn_up_s, L0_ffn_down_s):
    bf = ml_dtypes.bfloat16
    x = np.ascontiguousarray(np.asarray(L0_attn_x, np.float32).reshape(T, D))
    Wqkv = np.asarray(L0_attn_W_qkv, np.float32)
    Wout = np.asarray(L0_attn_W_out, np.float32)
    s_in = np.asarray(L0_attn_s, np.float32)
    gate_w = np.asarray(L0_ffn_gate_w, np.float32)
    up_w = np.asarray(L0_ffn_up_w, np.float32)
    down_w = np.asarray(L0_ffn_down_w, np.float32)
    gate_s = np.asarray(L0_ffn_gate_s, np.float32)
    up_s = np.asarray(L0_ffn_up_s, np.float32)
    down_s = np.asarray(L0_ffn_down_s, np.float32)

    shared = {}
    shared["wout_r"] = np.ascontiguousarray(
        Wout.reshape(16, 128, 16, 128).transpose(2, 1, 0, 3))
    shared["gw_r"] = np.ascontiguousarray(
        gate_w.reshape(NIT, 128, 16, 128).transpose(0, 3, 2, 1))
    shared["uw_r"] = np.ascontiguousarray(
        up_w.reshape(NIT, 128, 16, 128).transpose(0, 3, 2, 1))
    shared["dw_r"] = np.ascontiguousarray(
        down_w.reshape(16, 128, NIT, 128).transpose(0, 3, 2, 1))
    shared["gsc"] = np.ascontiguousarray(gate_s.reshape(NIT, 128).T)
    shared["usc"] = np.ascontiguousarray(up_s.reshape(NIT, 128).T)
    shared["dsc"] = np.ascontiguousarray(down_s.reshape(16, 128).T)
    PS = np.zeros((128, 128), np.float32)
    for b in range(2):
        B = 64 * b
        for i in range(32):
            PS[B + 32 + i, B + i] = 1.0
            PS[B + i, B + 32 + i] = -1.0
    shared["pswap"] = PS.astype(bf)
    shared["ident"] = np.eye(128, dtype=bf)
    shared["onesr"] = np.ones((1, 128), np.float32)
    shared["konst"] = np.broadcast_to(
        np.array([1e-5, -0.5, 0.125, 0.0], np.float32), (128, 4)).copy()

    cos, sin = _rope_tables()
    shared["cosg"] = np.ascontiguousarray(np.tile(cos.T, (4, 1)))
    shared["sing"] = np.ascontiguousarray(np.tile(sin.T, (4, 1)))

    tk = np.arange(128)[:, None]
    xx = np.arange(2432)[None, :]
    bad = ((xx - 896 - tk) >= 0) & ((xx - 896 - tk) <= 511)
    shared["masks"] = np.ascontiguousarray(
        np.where(bad, 0.0, 1.0).astype(bf))

    in_maps = []
    for c in range(NCORES):
        m = dict(shared)
        xc = np.ascontiguousarray(x[TC * c:TC * (c + 1)])
        m["x_c"] = xc
        m["xT_c"] = np.ascontiguousarray(xc.T)
        wk = Wqkv[:, 2048 + 64 * c:2048 + 64 * (c + 1)]
        wv = Wqkv[:, 2560 + 64 * c:2560 + 64 * (c + 1)]
        m["wkv_r"] = np.ascontiguousarray(
            np.concatenate([wk, wv], axis=1).reshape(16, 128, 128))
        m["wq_r"] = np.ascontiguousarray(
            Wqkv[:, 256 * c:256 * (c + 1)].reshape(16, 128, 256))
        m["sink"] = np.ascontiguousarray(s_in.reshape(NKV, QM)[c:c + 1, :])
        in_maps.append(m)
    return in_maps


def kernel(**inputs):
    in_maps = _host_prep(**inputs)
    nc = build_nc()
    res = run_bass_kernel_spmd(nc, in_maps, core_ids=list(range(NCORES)))
    out = np.concatenate(
        [np.asarray(r["out_d"], np.float32).T for r in res.results], axis=0)
    return out.reshape(1, T, D)


# version 18 (final)
# speedup vs baseline: 1.1388x; 1.1388x over previous
import sys
for _p in ("/opt/trn_rl_repo",):
    if _p not in sys.path:
        sys.path.append(_p)
import numpy as np
import ml_dtypes
from contextlib import ExitStack

import concourse.bass as bass
import concourse.mybir as mybir
from concourse.bass_utils import run_bass_kernel_spmd
from concourse.aot_env import allow_ici_all_to_all

F32 = mybir.dt.float32
BF16 = mybir.dt.bfloat16
AF = mybir.ActivationFunctionType
ALU = mybir.AluOpType

NCORES = 8
T = 8192
TC = T // NCORES
D = 2048
NKV, QM, HD = 8, 4, 64
INTER = 5632
NIT = INTER // 128
NDT = D // 128
NOC = TC // 128
NTB = T // 512
NQB = T // 1024
NOB = 32
OB_SPLIT = 22

VE_CH0 = 9
VE_CHK = 15
PE_CHK = 7
POP = 6
GS_N = NOB * 32


class Sem:
    def __init__(self, ctx, nc, name):
        self.h = ctx.enter_context(nc.semaphore(name))
        self.n = 0


def masked_pi(qb, sp):
    pi = sp - 4 * qb + 2
    return pi if 0 <= pi < 6 else None


def _ffn_schedule():
    items = []
    for it in range(NIT):
        av_g = it * 12
        for k in range(32):
            items.append(["g", it, k, av_g])
        for k in range(32):
            items.append(["u", it, k, None])
    for it2 in range(32):
        if it2 < 16:
            av = 712 + it2 * 11
        else:
            av = max(916, 712 + it2 * 11)
        for k in range(NIT):
            items.append(["d", it2, k, av])

    sched = [[] for _ in range(GS_N)]
    silu_at = {}
    hdn_at = {}
    act2_at = {}
    qi = 0
    for gs in range(GS_N):
        budget = POP
        while qi < len(items) and budget > 0:
            kind, idx, k, av = items[qi]
            if av is None or av > gs:
                break
            sched[gs].append(qi)
            if kind == "g" and k == 31:
                sgs = gs + 1
                silu_at.setdefault(sgs, []).append(idx)
                for q2 in range(qi + 1, qi + 33):
                    if items[q2][0] == "u" and items[q2][1] == idx:
                        items[q2][3] = sgs + 1
            if kind == "u" and k == 31:
                hdn_at.setdefault(gs + 1, []).append(idx)
                for q2 in range(qi + 1, min(qi + 33, len(items))):
                    if items[q2][0] == "g" and items[q2][1] == idx + 1:
                        items[q2][3] = max(items[q2][3], gs + 2)
            if kind == "d" and k == NIT - 1 and idx % 2 == 1:
                act2_at.setdefault(gs + 1, []).extend([idx - 1, idx])
                for q2 in range(qi + 1, min(qi + 1 + NIT, len(items))):
                    if items[q2][0] == "d" and items[q2][1] == idx + 1:
                        items[q2][3] = max(items[q2][3], gs + 2)
            qi += 1
            budget -= 1
    tail = list(range(qi, len(items)))
    sched_acts = {i2 for lst in act2_at.values() for i2 in lst}
    tail_acts = sorted(set(range(32)) - sched_acts)
    lim = OB_SPLIT * 32
    for i, (kind, idx, k, av) in enumerate(items):
        if kind in ("g", "u"):
            assert i < qi, f"gate/up item {i} unscheduled"
    for gdict in (silu_at, hdn_at):
        for gk in gdict:
            assert gk < lim, f"gate/up followup at gs {gk} >= {lim}"
    for gs in range(lim, GS_N):
        for i in sched[gs]:
            assert items[i][0] == "d"
    for gk in act2_at:
        assert gk < GS_N + 64
    return items, sched, silu_at, hdn_at, act2_at, tail, tail_acts


def build_nc(dbg=False):
    nc = bass.Bass("TRN2", target_bir_lowering=False)

    def din(name, shape, dt=F32):
        return nc.dram_tensor(name, list(shape), dt, kind="ExternalInput").ap()

    x_c = din("x_c", [TC, D])
    xT_c = din("xT_c", [D, TC])
    cosg = din("cosg", [128, T])
    sing = din("sing", [128, T])
    masks = din("masks", [128, 2432], BF16)
    wkv_r = din("wkv_r", [16, 128, 128])
    wq_r = din("wq_r", [16, 128, 256])
    wout_r = din("wout_r", [16, 128, 16, 128])
    gw_r = din("gw_r", [NIT, 128, 16, 128])
    uw_r = din("uw_r", [NIT, 128, 16, 128])
    dw_r = din("dw_r", [16, 128, NIT, 128])
    gsc = din("gsc", [128, NIT])
    usc = din("usc", [128, NIT])
    dsc = din("dsc", [128, 16])
    sink = din("sink", [1, 4])
    pswap = din("pswap", [128, 128], BF16)
    ident = din("ident", [128, 128], BF16)
    onesr = din("onesr", [1, 128])
    konst = din("konst", [128, 4])

    out_d = nc.dram_tensor("out_d", [D, TC], F32, kind="ExternalOutput").ap()

    stage_xn = nc.dram_tensor("stage_xn", [NDT * NOC * 128, 128], BF16).ap()
    xn_all = nc.dram_tensor("xn_all", [NCORES * NDT * NOC * 128, 128], BF16,
                            addr_space="Shared").ap()
    q_d = nc.dram_tensor("q_d", [128, 2 * T], BF16).ap()
    hdn_d = nc.dram_tensor("hdn_d", [NIT, 128, 1024], BF16).ap()
    ctx_stage = nc.dram_tensor("ctx_stage", [D, TC], BF16).ap()
    ctx_all = nc.dram_tensor("ctx_all", [D, TC], BF16).ap()

    items, sched, silu_at, hdn_at, act2_at, tail_items, tail_acts = _ffn_schedule()

    ctx = ExitStack()
    with ctx:
        sb = lambda n, s, d: ctx.enter_context(nc.sbuf_tensor(n, list(s), d))
        sem = lambda n: Sem(ctx, nc, n)
        block = ctx.enter_context(nc.Block())

        xn_T = sb("xn_T", [128, NOC * 2048], BF16)
        g_T = sb("g_T", [128, 16 * 1024], BF16)
        ctx_sb = sb("ctx_sb", [128, 2 * T], BF16)
        KT = sb("KT", [128, 4096], BF16)
        VS = sb("VS", [128, 64 * 65], BF16)
        gsc_sb = sb("gsc_sb", [128, NIT], F32)
        usc_sb = sb("usc_sb", [128, NIT], F32)
        dsc_sb = sb("dsc_sb", [128, 16], F32)
        gsch_sb = sb("gsch_sb", [128, NIT], F32)
        pswap_sb = sb("pswap_sb", [128, 128], BF16)
        ident_sb = sb("ident_sb", [128, 128], BF16)
        ones_sb = sb("ones_sb", [1, 128], F32)
        sink_sb = sb("sink_sb", [1, 4], F32)
        esink_sb = sb("esink_sb", [1, 4], F32)
        konst_sb = sb("konst_sb", [128, 4], F32)

        LDC = sem("LDC")
        CCX = sem("CCX")
        CCA = sem("CCA")

        def xn_rhs(dt, n2):
            return xn_T[:, dt * 1024 + n2 * 512:dt * 1024 + (n2 + 1) * 512]

        @block.gpsimd
        def _(g):
            for dst, src in ((gsc_sb, gsc), (usc_sb, usc), (dsc_sb, dsc),
                             (pswap_sb, pswap), (ident_sb, ident),
                             (ones_sb, onesr), (sink_sb, sink),
                             (konst_sb, konst)):
                g.dma_start(out=dst[:], in_=src).then_inc(LDC.h, 16)
                LDC.n += 16

        with ExitStack() as p0:
            sb0 = lambda n, s, d: p0.enter_context(nc.sbuf_tensor(n, list(s), d))
            ps0 = lambda n, s, d: p0.enter_context(nc.psum_tensor(n, list(s), d))
            xch = [sb0(f"xch{i}", [128, 2048], F32) for i in range(2)]
            junk = sb0("junk0", [128, 2048], F32)
            junk2 = sb0("junk2", [128, 2048], F32)
            stat = sb0("stat", [128, 16], F32)
            xnm = sb0("xnm", [128, 2048], BF16)
            xnchT = [sb0(f"xnchT{i}", [128, 2048], BF16) for i in range(2)]
            xnT_ps = ps0("xnT_ps", [128, 2048], BF16)

            def b_pos(ct, op):
                base = 9 * ct + 10 if ct < NOC - 1 else 9 * (NOC - 1) + 5
                return base + (op - 5)

            LDX = [sem("LDX0"), sem("LDX1")]
            VE0, PE0, SE0 = sem("VE0"), sem("PE0"), sem("SE0")
            STX = [sem("STX0"), sem("STX1")]

            @block.sync
            def _(sy):
                for ct in range(NOC):
                    if ct >= 2:
                        sy.wait_ge(VE0.h, b_pos(ct - 2, 7))
                    sy.dma_start(out=xch[ct % 2][:],
                                 in_=x_c[ct * 128:(ct + 1) * 128, :]).then_inc(LDX[ct % 2].h, 16)
                    LDX[ct % 2].n += 16

            @block.scalar
            def _(s):
                s.wait_ge(LDC.h, LDC.n)
                for ct in range(NOC):
                    o = (ct % 2) * 8
                    s.wait_ge(VE0.h, ct * VE_CH0 + 5)
                    ins = s.activation(stat[:, o + 7:o + 8], stat[:, o + 4:o + 5], AF.Ln,
                                       bias=konst_sb[:, 0:1])
                    ins.then_inc(SE0.h, 1); SE0.n += 1
                    s.wait_ge(SE0.h, ct * 2 + 1)
                    ins = s.activation(stat[:, o + 5:o + 6], stat[:, o + 7:o + 8], AF.Exp,
                                       scale=konst_sb[:, 1:2])
                    ins.then_inc(SE0.h, 1); SE0.n += 1

            @block.vector
            def _(v):
                def vdo(ins):
                    ins.then_inc(VE0.h, 1); VE0.n += 1
                    return ins

                def phase_a(ct):
                    o = (ct % 2) * 8
                    xc = xch[ct % 2]
                    v.wait_ge(LDX[ct % 2].h, (ct // 2 + 1) * 16)
                    vdo(v.tensor_scalar(junk[:], xc[:], 1.0, 0.0, op0=ALU.mult,
                                        op1=ALU.add, accum_out=stat[:, o:o + 1]))
                    vdo(v.scalar_tensor_tensor(junk2[:], xc[:], 1.0, xc[:], op0=ALU.mult,
                                               op1=ALU.mult, accum_out=stat[:, o + 1:o + 2]))
                    v.wait_ge(VE0.h, VE0.n)
                    vdo(v.tensor_scalar(stat[:, o + 2:o + 3], stat[:, o:o + 1],
                                        1.0 / D, None, op0=ALU.mult))
                    v.wait_ge(VE0.h, VE0.n)
                    vdo(v.tensor_tensor(stat[:, o + 3:o + 4], stat[:, o + 2:o + 3],
                                        stat[:, o + 2:o + 3], op=ALU.mult))
                    v.wait_ge(VE0.h, VE0.n)
                    vdo(v.scalar_tensor_tensor(stat[:, o + 4:o + 5], stat[:, o + 1:o + 2],
                                               1.0 / D, stat[:, o + 3:o + 4],
                                               op0=ALU.mult, op1=ALU.subtract))

                def phase_b(ct):
                    o = (ct % 2) * 8
                    xc = xch[ct % 2]
                    v.wait_ge(SE0.h, (ct + 1) * 2)
                    vdo(v.tensor_scalar(stat[:, o + 6:o + 7], stat[:, o + 2:o + 3],
                                        stat[:, o + 5:o + 6], -1.0,
                                        op0=ALU.mult, op1=ALU.mult))
                    v.wait_ge(VE0.h, VE0.n)
                    vdo(v.tensor_scalar(xnm[:], xc[:], stat[:, o + 5:o + 6],
                                        stat[:, o + 6:o + 7],
                                        op0=ALU.mult, op1=ALU.add))
                    v.wait_ge(PE0.h, ct + 1)
                    if ct >= 2:
                        v.wait_ge(STX[ct % 2].h, (ct // 2) * 16)
                    vdo(v.tensor_copy(xnchT[ct % 2][:], xnT_ps[:]))
                    vdo(v.tensor_copy(
                        xn_T[:].rearrange("p (dt ct2 t) -> p dt ct2 t",
                                          dt=NDT, ct2=NOC)[:, :, ct, :],
                        xnT_ps[:].rearrange("p (dt t) -> p dt t", dt=NDT)))

                for ct in range(NOC):
                    phase_a(ct)
                    if ct >= 1:
                        phase_b(ct - 1)
                phase_b(NOC - 1)

            @block.tensor
            def _(t):
                t.wait_ge(LDC.h, LDC.n)
                for ct in range(NOC):
                    t.wait_ge(VE0.h, b_pos(ct, 7))
                    if ct >= 1:
                        t.wait_ge(VE0.h, b_pos(ct - 1, 9))
                    for dt in range(NDT):
                        ins = t.transpose(xnT_ps[:, dt * 128:(dt + 1) * 128],
                                          xnm[:, dt * 128:(dt + 1) * 128], ident_sb[:])
                    ins.then_inc(PE0.h, 1); PE0.n += 1

            sxr = stage_xn.rearrange("(dt ct p) t -> ct p dt t", dt=NDT, ct=NOC)

            @block.gpsimd
            def _(g):
                for ct in range(NOC):
                    g.wait_ge(VE0.h, b_pos(ct, 8))
                    g.dma_start(out=sxr[ct],
                                in_=xnchT[ct % 2][:].rearrange(
                                    "p (dt t) -> p dt t", dt=NDT)).then_inc(STX[ct % 2].h, 16)
                    STX[ct % 2].n += 16
                g.wait_ge(STX[0].h, STX[0].n)
                g.wait_ge(STX[1].h, STX[1].n)

            p0_final = dict(VE0=(VE0.h, VE0.n), PE0=(PE0.h, PE0.n),
                            SE0=(SE0.h, SE0.n), STX0=(STX[0].h, STX[0].n),
                            STX1=(STX[1].h, STX[1].n))

        with ExitStack() as pk:
            sbk = lambda n, s, d: pk.enter_context(nc.sbuf_tensor(n, list(s), d))
            psk = lambda n, s, d: pk.enter_context(nc.psum_tensor(n, list(s), d))
            xnW = [sbk(f"xnW{i}", [128, NDT * 512], BF16) for i in range(2)]
            wkv_sb = sbk("wkv_sb", [128, 16 * 128], BF16)
            wq_sb = sbk("wq_sb", [128, 16 * 256], BF16)
            cosQ = [sbk(f"cosQ{i}", [128, 512], F32) for i in range(2)]
            sinQ = [sbk(f"sinQ{i}", [128, 512], F32) for i in range(2)]
            kbf = sbk("kbf", [64, 512], BF16)
            vbf = sbk("vbf", [64, 512], BF16)
            t1s = sbk("t1s", [64, 512], F32)
            t2s = sbk("t2s", [64, 512], F32)
            qbf = sbk("qbf", [128, 512], BF16)
            t1q = sbk("t1q", [128, 512], F32)
            t2q = sbk("t2q", [128, 512], F32)
            qstage = [sbk(f"qstage{i}", [128, 512], BF16) for i in range(2)]
            kvps = psk("kvps", [128, 512], F32)
            ksw = psk("ksw", [64, 512], F32)
            vT_ps = psk("vT_ps", [128, 256], BF16)
            qp = [psk(f"qp{i}", [128, 512], F32) for i in range(2)]
            qsw = psk("qsw", [128, 512], F32)

            LXW = [sem("LXWa"), sem("LXWb")]
            LWV, LRT = sem("LWV"), sem("LRT")
            PEK, VEK = sem("PEK"), sem("VEK")
            STQ = [sem("STQa"), sem("STQb")]

            xar = xn_all.rearrange("(rk dt ct p) t -> rk dt ct p t",
                                   rk=NCORES, dt=NDT, ct=NOC)

            @block.gpsimd
            def _(g):
                for hs, vv in p0_final.items():
                    g.wait_ge(vv[0], vv[1])
                g.dma_start(out=wkv_sb[:].rearrange("p (dt c) -> p dt c", dt=16),
                            in_=wkv_r.rearrange("dt p c -> p dt c")).then_inc(LWV.h, 16)
                g.dma_start(out=wq_sb[:].rearrange("p (dt c) -> p dt c", dt=16),
                            in_=wq_r.rearrange("dt p c -> p dt c")).then_inc(LWV.h, 16)
                LWV.n += 32
                g.collective_compute(
                    "AllGather", ALU.bypass,
                    replica_groups=[list(range(NCORES))],
                    ins=[stage_xn.tensor.ap().opt()],
                    outs=[xn_all.tensor.ap().opt()],
                ).then_inc(CCX.h)
                CCX.n = 1

            @block.sync
            def _(sy):
                for hs, vv in p0_final.items():
                    sy.wait_ge(vv[0], vv[1])
                sy.wait_ge(CCX.h, 1)
                for tb in range(NTB):
                    rk, half = tb // 2, tb % 2
                    if tb >= 2:
                        sy.wait_ge(PEK.h, (tb - 1) * PE_CHK)
                    for dt in range(NDT):
                        src = xar[rk, dt, 4 * half:4 * half + 4, :, :]
                        sy.dma_start(
                            out=xnW[tb % 2][:, dt * 512:(dt + 1) * 512],
                            in_=src.rearrange("ct p t -> p ct t"),
                        ).then_inc(LXW[tb % 2].h, 16)
                        LXW[tb % 2].n += 16
                    if tb >= 1:
                        sy.wait_ge(VEK.h, (tb - 1) * VE_CHK + 1)
                    sy.dma_start(out=cosQ[tb % 2][:],
                                 in_=cosg[:, tb * 512:(tb + 1) * 512]).then_inc(LRT.h, 16)
                    sy.dma_start(out=sinQ[tb % 2][:],
                                 in_=sing[:, tb * 512:(tb + 1) * 512]).then_inc(LRT.h, 16)
                    LRT.n += 32
                    if tb >= 1:
                        for mp in range(2):
                            sy.wait_ge(VEK.h, (tb - 1) * VE_CHK + 11 + 4 * mp)
                            sy.dma_start(out=q_d[:, mp * T + (tb - 1) * 512:mp * T + tb * 512],
                                         in_=qstage[mp][:]).then_inc(STQ[mp].h, 16)
                            STQ[mp].n += 16
                for mp in range(2):
                    sy.wait_ge(VEK.h, (NTB - 1) * VE_CHK + 11 + 4 * mp)
                    sy.dma_start(out=q_d[:, mp * T + (NTB - 1) * 512:mp * T + NTB * 512],
                                 in_=qstage[mp][:]).then_inc(STQ[mp].h, 16)
                    STQ[mp].n += 16

            @block.tensor
            def _(t):
                for hs, vv in p0_final.items():
                    t.wait_ge(vv[0], vv[1])
                t.wait_ge(LWV.h, 32)
                for tb in range(NTB):
                    t.wait_ge(LXW[tb % 2].h, (tb // 2 + 1) * 256)
                    xw = xnW[tb % 2]
                    if tb >= 1:
                        t.wait_ge(VEK.h, (tb - 1) * VE_CHK + 4)
                    for dt in range(NDT):
                        ins = t.matmul(kvps[:],
                                       wkv_sb[:, dt * 128:(dt + 1) * 128],
                                       xw[:, dt * 512:(dt + 1) * 512],
                                       start=(dt == 0), stop=(dt == NDT - 1))
                    ins.then_inc(PEK.h, 1); PEK.n += 1
                    t.wait_ge(VEK.h, tb * VE_CHK + 1)
                    ins = t.matmul(ksw[:], pswap_sb[0:64, 0:64], kbf[:],
                                   start=True, stop=True)
                    ins.then_inc(PEK.h, 1); PEK.n += 1
                    t.wait_ge(VEK.h, tb * VE_CHK + 2)
                    if tb >= 1:
                        t.wait_ge(VEK.h, (tb - 1) * VE_CHK + 7)
                    for gi in range(4):
                        ins = t.transpose(vT_ps[:, gi * 64:(gi + 1) * 64],
                                          vbf[:, gi * 128:(gi + 1) * 128],
                                          ident_sb[0:64, 0:64])
                    ins.then_inc(PEK.h, 1); PEK.n += 1
                    for mp in range(2):
                        if tb >= 1:
                            t.wait_ge(VEK.h, (tb - 1) * VE_CHK + 10 + 4 * mp)
                        for dt in range(NDT):
                            ins = t.matmul(qp[mp][:],
                                           wq_sb[:, dt * 256 + mp * 128:dt * 256 + (mp + 1) * 128],
                                           xw[:, dt * 512:(dt + 1) * 512],
                                           start=(dt == 0), stop=(dt == NDT - 1))
                        ins.then_inc(PEK.h, 1); PEK.n += 1
                        t.wait_ge(VEK.h, tb * VE_CHK + 8 + mp * 4)
                        if tb >= 1 and mp == 0:
                            t.wait_ge(VEK.h, (tb - 1) * VE_CHK + 13)
                        if mp == 1:
                            t.wait_ge(VEK.h, tb * VE_CHK + 9)
                        ins = t.matmul(qsw[:], pswap_sb[:], qbf[:],
                                       start=True, stop=True)
                        ins.then_inc(PEK.h, 1); PEK.n += 1

            @block.vector
            def _(v):
                def vdo(ins):
                    ins.then_inc(VEK.h, 1); VEK.n += 1
                    return ins
                v.memset(VS[:].rearrange("k (g c) -> k g c", c=65)[:, :, 64:65], 1.0)
                for hs, vv in p0_final.items():
                    v.wait_ge(vv[0], vv[1])
                for tb in range(NTB):
                    v.wait_ge(LRT.h, (tb + 1) * 32)
                    v.wait_ge(PEK.h, tb * PE_CHK + 1)
                    vdo(v.tensor_copy(kbf[:], kvps[0:64, :]))
                    vdo(v.tensor_copy(vbf[:], kvps[64:128, :]))
                    v.wait_ge(PEK.h, tb * PE_CHK + 2)
                    vdo(v.tensor_tensor(t1s[:], ksw[:], sinQ[tb % 2][0:64, :], op=ALU.mult))
                    vdo(v.tensor_tensor(t2s[:], kvps[0:64, :], cosQ[tb % 2][0:64, :], op=ALU.mult))
                    v.wait_ge(VEK.h, VEK.n)
                    t1v = t1s[:].rearrange("p (gi2 two k) -> p two gi2 k", two=2, k=128)
                    t2v = t2s[:].rearrange("p (gi2 two k) -> p two gi2 k", two=2, k=128)
                    ktv = KT[:].rearrange("p (b k) -> p b k", k=128)
                    vdo(v.tensor_tensor(ktv[0:64, 2 * tb:2 * tb + 2, :],
                                        t1v[:, 0], t2v[:, 0], op=ALU.add))
                    vdo(v.tensor_tensor(ktv[64:128, 2 * tb:2 * tb + 2, :],
                                        t1v[:, 1], t2v[:, 1], op=ALU.add))
                    v.wait_ge(PEK.h, tb * PE_CHK + 3)
                    vdo(v.tensor_copy(
                        VS[:].rearrange("k (g c) -> k g c", c=65)
                        [:, 4 * tb:4 * tb + 4, 0:64],
                        vT_ps[:].rearrange("k (g c) -> k g c", c=64)))
                    for mp in range(2):
                        v.wait_ge(PEK.h, tb * PE_CHK + 4 + mp * 2)
                        vdo(v.tensor_copy(qbf[:], qp[mp][:]))
                        v.wait_ge(PEK.h, tb * PE_CHK + 5 + mp * 2)
                        vdo(v.tensor_tensor(t1q[:], qsw[:], sinQ[tb % 2][:], op=ALU.mult))
                        vdo(v.tensor_tensor(t2q[:], qp[mp][:], cosQ[tb % 2][:], op=ALU.mult))
                        v.wait_ge(VEK.h, VEK.n)
                        if tb >= 1:
                            v.wait_ge(STQ[mp].h, tb * 16)
                        vdo(v.tensor_tensor(qstage[mp][:], t1q[:], t2q[:], op=ALU.add))

            pkv_final = dict(PEK=(PEK.h, PEK.n), VEK=(VEK.h, VEK.n),
                             STQa=(STQ[0].h, STQ[0].n), STQb=(STQ[1].h, STQ[1].n),
                             LXWa=(LXW[0].h, LXW[0].n), LXWb=(LXW[1].h, LXW[1].n))

        with ExitStack() as b2c:
            sbc = lambda n, s, d: b2c.enter_context(nc.sbuf_tensor(n, list(s), d))
            psc = lambda n, s, d: b2c.enter_context(nc.psum_tensor(n, list(s), d))
            masks_sb = sbc("masks_sb", [128, 2432], BF16)
            QS = [sbc(f"QS{i}", [128, 1024], BF16) for i in range(2)]
            ES = [sbc(f"ES{i}", [128, 1024], BF16) for i in range(4)]
            dsum = sbc("dsum", [1, 1024], F32)
            rbc = sbc("rbc", [128, 1024], F32)
            scps = [psc(f"scps{i}", [128, 1024], F32) for i in range(2)]
            ctxp = psc("ctxp", [65, 1024], F32)

            LDM = sem("LDM")
            LQS = [sem("LQSa"), sem("LQSb")]
            QKs, EXs, MKs, WVs, BCs, VE4, SE4 = (sem("QKs"), sem("EXs"), sem("MKs"),
                                                 sem("WVs"), sem("BCs"), sem("VE4"),
                                                 sem("SE4"))
            LD1 = [sem("LD1a"), sem("LD1b")]
            STH = [sem("STHa"), sem("STHb")]
            PE1, VE1, SE1 = sem("PE1"), sem("VE1"), sem("SE1")
            LD2 = [sem("LD2a"), sem("LD2b")]
            LDH = sem("LDH")
            PE2, SE2 = sem("PE2"), sem("SE2")
            GSH = sem("GSH")
            gux_at = {gk + 1: lst for gk, lst in silu_at.items()}

            mk_after = {}
            _cnt = 0
            for _ob in range(NOB):
                _qb = _ob % NQB
                for _sp in range(32):
                    for _par in range(2):
                        _j = _ob * 64 + _sp * 2 + _par
                        if masked_pi(_qb, _sp) is not None:
                            _cnt += 1
                        mk_after[_j] = _cnt

            fstate = {}

            def emit_item(t, idx):
                kind, it, k, av = items[idx]
                if kind == "g":
                    if k == 0:
                        t.wait_ge(LD1[it % 2].h, (it // 2 + 1) * 32)
                        if it >= 1:
                            t.wait_ge(VE1.h, 3 * it)
                    dt, n2 = k // 2, k % 2
                    ins = t.matmul(fstate["gu_ps"][:, n2 * 512:(n2 + 1) * 512],
                                   fstate["gw"][it % 2][:, dt * 128:(dt + 1) * 128],
                                   xn_rhs(dt, n2),
                                   start=(dt == 0), stop=(dt == NDT - 1))
                    if k == 31:
                        ins.then_inc(PE1.h, 1); PE1.n += 1
                elif kind == "u":
                    if k == 0:
                        t.wait_ge(SE1.h, it + 1)
                        t.wait_ge(VE1.h, 3 * it + 1)
                    dt, n2 = k // 2, k % 2
                    ins = t.matmul(fstate["gu_ps"][:, n2 * 512:(n2 + 1) * 512],
                                   fstate["uw"][it % 2][:, dt * 128:(dt + 1) * 128],
                                   xn_rhs(dt, n2),
                                   start=(dt == 0), stop=(dt == NDT - 1))
                    if k == 31:
                        ins.then_inc(PE1.h, 1); PE1.n += 1
                else:
                    it2 = it
                    if k == 0:
                        th = it2 // 16
                        t.wait_ge(LD2[it2 % 2].h, (it2 // 2 + 1) * 16)
                        t.wait_ge(LDH.h, (th + 1) * 64)
                        if it2 >= 2:
                            t.wait_ge(SE2.h, it2 - 1)
                    ins = t.matmul(fstate["d_ps"][it2 % 2][:],
                                   fstate["dw"][it2 % 2][:, k * 128:(k + 1) * 128],
                                   fstate["hdn_half"][:, k * 512:(k + 1) * 512],
                                   start=(k == 0), stop=(k == NIT - 1))
                    if k == NIT - 1:
                        ins.then_inc(PE2.h, 1); PE2.n += 1

            def emit_wv(t, ob, sp, par):
                j = ob * 64 + sp * 2 + par
                qb = ob % NQB
                if masked_pi(qb, sp) is not None:
                    t.wait_ge(MKs.h, mk_after[j])
                else:
                    t.wait_ge(EXs.h, j + 1)
                if sp == 0 and par == 0:
                    t.wait_ge(VE4.h, ob * 4)
                ls = sp * 2 + par
                for qh in range(2):
                    ins = t.matmul(ctxp[:, qh * 512:(qh + 1) * 512],
                                   VS[:, ls * 65:ls * 65 + 65],
                                   ES[j % 4][:, qh * 512:(qh + 1) * 512],
                                   start=(sp == 0 and par == 0),
                                   stop=(sp == 31 and par == 1))
                ins.then_inc(WVs.h, 1); WVs.n += 1

            def emit_ob_tensor(t, ob):
                t.wait_ge(LQS[ob % 2].h, (ob // 2 + 1) * 32)
                for sp in range(32):
                    gs = ob * 32 + sp
                    j0 = ob * 64 + sp * 2
                    if j0 >= 2:
                        t.wait_ge(EXs.h, j0 - 1)
                    if sp == 0 and ob >= 1:
                        t.wait_ge(VE4.h, (ob - 1) * 4 + 3)
                    for qh in range(2):
                        ins = t.matmul(scps[0][:, qh * 512:(qh + 1) * 512],
                                       KT[0:64, sp * 128:(sp + 1) * 128],
                                       QS[ob % 2][0:64, qh * 512:(qh + 1) * 512],
                                       start=True, stop=True, tile_position=(0, 0))
                    ins.then_inc(QKs.h, 1); QKs.n += 1
                    if j0 >= 1:
                        t.wait_ge(EXs.h, j0)
                    for qh in range(2):
                        ins = t.matmul(scps[1][:, qh * 512:(qh + 1) * 512],
                                       KT[64:128, sp * 128:(sp + 1) * 128],
                                       QS[ob % 2][64:128, qh * 512:(qh + 1) * 512],
                                       start=True, stop=True, tile_position=(64, 0))
                    ins.then_inc(QKs.h, 1); QKs.n += 1
                    for idx in sched[gs]:
                        emit_item(t, idx)
                    if sp >= 1:
                        emit_wv(t, ob, sp - 1, 0)
                        emit_wv(t, ob, sp - 1, 1)
                emit_wv(t, ob, 31, 0)
                emit_wv(t, ob, 31, 1)
                t.wait_ge(VE4.h, ob * 4 + 2)
                t.wait_ge(EXs.h, ob * 64 + 63)
                for qh in range(2):
                    ins = t.matmul(scps[0][:, qh * 512:(qh + 1) * 512],
                                   ones_sb[:],
                                   dsum[0:1, qh * 512:(qh + 1) * 512],
                                   start=True, stop=True)
                ins.then_inc(BCs.h, 1); BCs.n += 1

            def emit_ob_scalar(s, ob):
                for sp in range(32):
                    gs = ob * 32 + sp
                    for par in range(2):
                        j = ob * 64 + sp * 2 + par
                        s.wait_ge(QKs.h, j + 1)
                        if j >= 4:
                            s.wait_ge(WVs.h, j - 3)
                        ins = s.activation(ES[j % 4][:], scps[par][:],
                                           AF.Exp, scale=konst_sb[:, 2:3])
                        ins.then_inc(EXs.h, 1); EXs.n += 1
                    for it in silu_at.get(gs, ()):
                        s.wait_ge(PE1.h, it * 2 + 1)
                        if it >= 1:
                            s.wait_ge(VE1.h, 3 * it)
                        ins = s.activation(fstate["sil"][:], fstate["gu_ps"][:],
                                           AF.Tanh, scale=gsch_sb[:, it:it + 1])
                        ins.then_inc(SE1.h, 1); SE1.n += 1
                    for it2 in act2_at.get(gs, ()):
                        th, dmt = it2 // 16, it2 % 16
                        s.wait_ge(PE2.h, it2 + 1)
                        ins = s.activation(
                            g_T[:, dmt * 1024 + th * 512:dmt * 1024 + th * 512 + 512],
                            fstate["d_ps"][it2 % 2][:], AF.Silu,
                            scale=dsc_sb[:, dmt:dmt + 1])
                        ins.then_inc(SE2.h, 1); SE2.n += 1

            def emit_ob_vector(v, ob):
                m, qb = ob // NQB, ob % NQB
                for sp in range(32):
                    gs = ob * 32 + sp
                    for it in gux_at.get(gs, ()):
                        v.wait_ge(PE1.h, it * 2 + 1)
                        v.wait_ge(VE1.h, VE1.n)
                        ins = v.tensor_scalar(fstate["gprod"][:], fstate["gu_ps"][:],
                                              gsch_sb[:, it:it + 1], 0.0,
                                              op0=ALU.mult, op1=ALU.add)
                        ins.then_inc(VE1.h, 1); VE1.n += 1
                        v.wait_ge(SE1.h, it + 1)
                        v.wait_ge(VE1.h, VE1.n)
                        ins = v.scalar_tensor_tensor(fstate["sil"][:], fstate["sil"][:],
                                                     1.0, fstate["gprod"][:],
                                                     op0=ALU.add, op1=ALU.mult)
                        ins.then_inc(VE1.h, 1); VE1.n += 1
                    for it in hdn_at.get(gs, ()):
                        v.wait_ge(PE1.h, it * 2 + 2)
                        v.wait_ge(VE1.h, 3 * it + 2)
                        if it >= 2:
                            v.wait_ge(STH[it % 2].h, (it // 2) * 16)
                        ins = v.scalar_tensor_tensor(
                            fstate["hdnst"][it % 2][:], fstate["gu_ps"][:],
                            usc_sb[:, it:it + 1], fstate["sil"][:],
                            op0=ALU.mult, op1=ALU.mult)
                        ins.then_inc(VE1.h, 1); VE1.n += 1
                    pi = masked_pi(qb, sp)
                    if pi is not None:
                        for par in range(2):
                            j = ob * 64 + sp * 2 + par
                            dlt = 2 * (sp - 4 * qb) + par
                            v.wait_ge(EXs.h, j + 1)
                            ins = v.tensor_tensor(
                                ES[j % 4][:], ES[j % 4][:],
                                masks_sb[:, 896 - 128 * dlt:896 - 128 * dlt + 1024],
                                op=ALU.mult)
                            ins.then_inc(MKs.h, 1); MKs.n += 1
                def vdo4(ins):
                    ins.then_inc(VE4.h, 1); VE4.n += 1
                    return ins
                v.wait_ge(WVs.h, (ob + 1) * 64)
                vdo4(v.tensor_copy(dsum[:], ctxp[64:65, :]))
                v.wait_ge(VE4.h, VE4.n)
                vdo4(v.tensor_scalar(dsum[:], dsum[:],
                                     esink_sb[0:1, m:m + 1],
                                     None, op0=ALU.add))
                v.wait_ge(BCs.h, ob + 1)
                vdo4(v.reciprocal(rbc[:], scps[0][:]))
                v.wait_ge(VE4.h, VE4.n)
                vdo4(v.tensor_tensor(
                    ctx_sb[(m % 2) * 64:(m % 2) * 64 + 64,
                           (m // 2) * T + qb * 1024:(m // 2) * T + (qb + 1) * 1024],
                    ctxp[0:64, :], rbc[0:64, :], op=ALU.mult))

            def emit_ob_sync(sy, ob):
                m, qb = ob // NQB, ob % NQB
                if ob >= 2:
                    sy.wait_ge(QKs.h, (ob - 1) * 64)
                half = q_d[(m % 2) * 64:(m % 2) * 64 + 64,
                           (m // 2) * T + qb * 1024:(m // 2) * T + (qb + 1) * 1024]
                sy.dma_start(out=QS[ob % 2][0:64, :], in_=half).then_inc(LQS[ob % 2].h, 16)
                sy.dma_start(out=QS[ob % 2][64:128, :], in_=half).then_inc(LQS[ob % 2].h, 16)
                LQS[ob % 2].n += 32

            with ExitStack() as b2a:
                sba = lambda n, s, d: b2a.enter_context(nc.sbuf_tensor(n, list(s), d))
                fstate["gw"] = [sba(f"gw{i}", [128, 2048], BF16) for i in range(2)]
                fstate["uw"] = [sba(f"uw{i}", [128, 2048], BF16) for i in range(2)]
                fstate["sil"] = sba("sil", [128, 1024], F32)
                fstate["gprod"] = sba("gprod", [128, 1024], F32)
                fstate["hdnst"] = [sba(f"hdnst{i}", [128, 1024], BF16) for i in range(2)]
                fstate["gu_ps"] = b2a.enter_context(nc.psum_tensor("gu_ps", [128, 1024], F32))

                @block.gpsimd
                def _(g):
                    for hs, vv in pkv_final.items():
                        g.wait_ge(vv[0], vv[1])
                    g.dma_start(out=masks_sb[:], in_=masks).then_inc(LDM.h, 16)
                    LDM.n += 16
                    for it in range(NIT):
                        if it >= 2:
                            g.wait_ge(PE1.h, (it - 1) * 2)
                        g.dma_start(out=fstate["gw"][it % 2][:].rearrange(
                            "p (dt c) -> p dt c", dt=16),
                            in_=gw_r[it]).then_inc(LD1[it % 2].h, 16)
                        LD1[it % 2].n += 16
                        g.dma_start(out=fstate["uw"][it % 2][:].rearrange(
                            "p (dt c) -> p dt c", dt=16),
                            in_=uw_r[it]).then_inc(LD1[it % 2].h, 16)
                        LD1[it % 2].n += 16
                        if it >= 2:
                            g.wait_ge(VE1.h, 3 * (it - 2) + 3)
                            g.dma_start(out=hdn_d[it - 2],
                                        in_=fstate["hdnst"][it % 2][:]).then_inc(STH[it % 2].h, 16)
                            STH[it % 2].n += 16
                    for it in (NIT - 2, NIT - 1):
                        g.wait_ge(VE1.h, 3 * it + 3)
                        g.dma_start(out=hdn_d[it],
                                    in_=fstate["hdnst"][it % 2][:]).then_inc(STH[it % 2].h, 16)
                        STH[it % 2].n += 16

                @block.scalar
                def _(s):
                    s.wait_ge(LDC.h, LDC.n)
                    s.wait_ge(GSH.h, 1)
                    ins = s.activation(esink_sb[:], sink_sb[:], AF.Exp)
                    ins.then_inc(SE4.h, 1); SE4.n += 1
                    for ob in range(OB_SPLIT):
                        emit_ob_scalar(s, ob)

                @block.sync
                def _(sy):
                    for hs, vv in pkv_final.items():
                        sy.wait_ge(vv[0], vv[1])
                    for ob in range(OB_SPLIT):
                        emit_ob_sync(sy, ob)

                @block.tensor
                def _(t):
                    for hs, vv in pkv_final.items():
                        t.wait_ge(vv[0], vv[1])
                    for ob in range(OB_SPLIT):
                        emit_ob_tensor(t, ob)

                @block.vector
                def _(v):
                    v.wait_ge(LDC.h, LDC.n)
                    ins = v.tensor_scalar(gsch_sb[:], gsc_sb[:], 0.5, None, op0=ALU.mult)
                    ins.then_inc(GSH.h, 1)
                    v.wait_ge(GSH.h, 1)
                    v.wait_ge(SE4.h, 1)
                    v.wait_ge(LDM.h, 16)
                    for ob in range(OB_SPLIT):
                        emit_ob_vector(v, ob)

            with ExitStack() as b2b:
                sbb = lambda n, s, d: b2b.enter_context(nc.sbuf_tensor(n, list(s), d))
                fstate["dw"] = [sbb(f"dw{i}", [128, NIT * 128], BF16) for i in range(2)]
                fstate["hdn_half"] = sbb("hdn_half", [128, NIT * 512], BF16)
                fstate["d_ps"] = [b2b.enter_context(
                    nc.psum_tensor(f"d_ps{i}", [128, 512], F32)) for i in range(2)]
                hdn_r = hdn_d.rearrange("it p t -> p it t")

                @block.gpsimd
                def _(g):
                    for th in range(2):
                        if th == 0:
                            g.wait_ge(STH[0].h, STH[0].n)
                            g.wait_ge(STH[1].h, STH[1].n)
                        else:
                            g.wait_ge(PE2.h, 16)
                        src = hdn_r[:, :, th * 512:(th + 1) * 512]
                        for q in range(4):
                            g.dma_start(out=fstate["hdn_half"]
                                        [:, q * 11 * 512:(q + 1) * 11 * 512]
                                        .rearrange("p (it t) -> p it t", it=11),
                                        in_=src[:, q * 11:(q + 1) * 11, :]).then_inc(LDH.h, 16)
                            LDH.n += 16
                        for it2 in range(th * 16, th * 16 + 16):
                            dmt = it2 % 16
                            if it2 >= 2:
                                g.wait_ge(PE2.h, it2 - 1)
                            g.dma_start(out=fstate["dw"][it2 % 2][:].rearrange(
                                "p (it c) -> p it c", it=NIT),
                                in_=dw_r[dmt]).then_inc(LD2[it2 % 2].h, 16)
                            LD2[it2 % 2].n += 16

                @block.scalar
                def _(s):
                    for ob in range(OB_SPLIT, NOB):
                        emit_ob_scalar(s, ob)
                    late = [it2 for gs in range(GS_N, GS_N + 64)
                            for it2 in act2_at.get(gs, ())] + tail_acts
                    for it2 in late:
                        th, dmt = it2 // 16, it2 % 16
                        s.wait_ge(PE2.h, it2 + 1)
                        ins = s.activation(
                            g_T[:, dmt * 1024 + th * 512:dmt * 1024 + th * 512 + 512],
                            fstate["d_ps"][it2 % 2][:], AF.Silu,
                            scale=dsc_sb[:, dmt:dmt + 1])
                        ins.then_inc(SE2.h, 1); SE2.n += 1

                @block.sync
                def _(sy):
                    for ob in range(OB_SPLIT, NOB):
                        emit_ob_sync(sy, ob)

                @block.tensor
                def _(t):
                    for ob in range(OB_SPLIT, NOB):
                        emit_ob_tensor(t, ob)
                    for idx in tail_items:
                        emit_item(t, idx)

                @block.vector
                def _(v):
                    for ob in range(OB_SPLIT, NOB):
                        emit_ob_vector(v, ob)

                down_final = dict(PE2=(PE2.h, PE2.n), SE2=(SE2.h, SE2.n))

            p2_final = dict(QKs=(QKs.h, QKs.n), EXs=(EXs.h, EXs.n),
                            WVs=(WVs.h, WVs.n), VE4=(VE4.h, VE4.n),
                            BCs=(BCs.h, BCs.n), MKs=(MKs.h, MKs.n))

        STC = sem("STC")

        @block.gpsimd
        def _(g):
            for hs, vv in p2_final.items():
                g.wait_ge(vv[0], vv[1])
            csr = ctx_stage.rearrange("(j rg p) t -> rg p j t", j=NCORES, rg=2)
            cbr = ctx_sb[:].rearrange("p (rg j t) -> p rg j t", rg=2, j=NCORES)
            for rg in range(2):
                g.dma_start(out=csr[rg], in_=cbr[:, rg]).then_inc(STC.h, 16)
                STC.n += 16
            g.wait_ge(STC.h, STC.n)
            with allow_ici_all_to_all():
                g.collective_compute(
                    "AllToAll", ALU.bypass,
                    replica_groups=[list(range(NCORES))],
                    ins=[ctx_stage.tensor.ap().opt()],
                    outs=[ctx_all.tensor.ap().opt()],
                ).then_inc(CCA.h)
            CCA.n = 1

        with ExitStack() as p3:
            sb4 = lambda n, s, d: p3.enter_context(nc.sbuf_tensor(n, list(s), d))
            ps4 = lambda n, s, d: p3.enter_context(nc.psum_tensor(n, list(s), d))
            ctxR = sb4("ctxR", [128, 16 * 1024], BF16)
            wo_sb = [sb4(f"wo{i}", [128, 2048], BF16) for i in range(2)]
            xt_sb = [sb4(f"xt{i}", [128, 1024], F32) for i in range(2)]
            prod = sb4("prod", [128, 1024], F32)
            outst = [sb4(f"outst{i}", [128, 1024], F32) for i in range(2)]
            s_ps = [ps4(f"s_ps{i}", [128, 1024], F32) for i in range(2)]

            LD5 = [sem("LD5a"), sem("LD5b")]
            LCX = sem("LCX")
            LX3 = [sem("LX3a"), sem("LX3b")]
            STO = [sem("STOa"), sem("STOb")]
            PE5, VE5 = sem("PE5"), sem("VE5")

            @block.gpsimd
            def _(g):
                for hs, vv in down_final.items():
                    g.wait_ge(vv[0], vv[1])
                g.wait_ge(CCA.h, 1)
                for dv in range(16):
                    g.dma_start(out=ctxR[:, dv * 1024:(dv + 1) * 1024],
                                in_=ctx_all[dv * 128:(dv + 1) * 128, :]).then_inc(LCX.h, 16)
                    LCX.n += 16
                for dmt in range(16):
                    if dmt >= 2:
                        g.wait_ge(PE5.h, dmt - 1)
                    g.dma_start(out=wo_sb[dmt % 2][:].rearrange("p (dv c) -> p dv c", dv=16),
                                in_=wout_r[dmt]).then_inc(LD5[dmt % 2].h, 16)
                    LD5[dmt % 2].n += 16

            def p3_store(sy, dmt):
                sy.wait_ge(VE5.h, dmt * 2 + 2)
                sy.dma_start(out=out_d[dmt * 128:(dmt + 1) * 128, :],
                             in_=outst[dmt % 2][:]).then_inc(STO[dmt % 2].h, 16)
                STO[dmt % 2].n += 16

            @block.sync
            def _(sy):
                for hs, vv in p2_final.items():
                    sy.wait_ge(vv[0], vv[1])
                for hs, vv in down_final.items():
                    sy.wait_ge(vv[0], vv[1])
                for dmt in range(16):
                    if dmt >= 2:
                        sy.wait_ge(VE5.h, (dmt - 1) * 2)
                    sy.dma_start(out=xt_sb[dmt % 2][:],
                                 in_=xT_c[dmt * 128:(dmt + 1) * 128, :]).then_inc(LX3[dmt % 2].h, 16)
                    LX3[dmt % 2].n += 16
                    if dmt >= 1:
                        p3_store(sy, dmt - 1)
                p3_store(sy, 15)
                sy.wait_ge(STO[0].h, 128)
                sy.wait_ge(STO[1].h, 128)

            @block.tensor
            def _(t):
                t.wait_ge(LCX.h, 256)
                for dmt in range(16):
                    t.wait_ge(LD5[dmt % 2].h, (dmt // 2 + 1) * 16)
                    if dmt >= 2:
                        t.wait_ge(VE5.h, (dmt - 1) * 2)
                    for dv in range(16):
                        for n2 in range(2):
                            ins = t.matmul(s_ps[dmt % 2][:, n2 * 512:(n2 + 1) * 512],
                                           wo_sb[dmt % 2][:, dv * 128:(dv + 1) * 128],
                                           ctxR[:, dv * 1024 + n2 * 512:dv * 1024 + (n2 + 1) * 512],
                                           start=(dv == 0), stop=(dv == 15))
                    ins.then_inc(PE5.h, 1); PE5.n += 1

            @block.vector
            def _(v):
                for hs, vv in down_final.items():
                    v.wait_ge(vv[0], vv[1])
                for dmt in range(16):
                    v.wait_ge(PE5.h, dmt + 1)
                    v.wait_ge(VE5.h, VE5.n)
                    ins = v.tensor_tensor(prod[:], s_ps[dmt % 2][:],
                                          g_T[:, dmt * 1024:(dmt + 1) * 1024], op=ALU.mult)
                    ins.then_inc(VE5.h, 1); VE5.n += 1
                    v.wait_ge(VE5.h, VE5.n)
                    v.wait_ge(LX3[dmt % 2].h, (dmt // 2 + 1) * 16)
                    if dmt >= 2:
                        v.wait_ge(STO[dmt % 2].h, (dmt // 2) * 16)
                    ins = v.tensor_tensor(outst[dmt % 2][:], prod[:], xt_sb[dmt % 2][:],
                                          op=ALU.add)
                    ins.then_inc(VE5.h, 1); VE5.n += 1

    return nc


def _rope_tables():
    import os, subprocess, tempfile
    prog = (
        "import numpy as np, jax.numpy as jnp, sys\n"
        "inv = (1.0 / 1024.0) ** jnp.linspace(0.0, 1.0, %d)\n"
        "inv = jnp.concatenate([inv, jnp.zeros(%d)])\n"
        "th = jnp.arange(%d, dtype=jnp.float32)[:, None] * inv[None, :]\n"
        "np.save(sys.argv[1] + 'c.npy', np.asarray(jnp.cos(th), np.float32))\n"
        "np.save(sys.argv[1] + 's.npy', np.asarray(jnp.sin(th), np.float32))\n"
        % (HD // 4, HD // 4, T))
    try:
        with tempfile.NamedTemporaryFile(suffix="_rope", delete=False) as f:
            path = f.name
        env = dict(os.environ)
        env.pop("TRN_TERMINAL_POOL_IPS", None)
        env["JAX_PLATFORMS"] = "cpu"
        npp = env.get("NIX_PYTHONPATH", "")
        if npp:
            env["PYTHONPATH"] = npp + os.pathsep + env.get("PYTHONPATH", "")
        r = subprocess.run([sys.executable, "-c", prog, path], env=env,
                           capture_output=True, timeout=600)
        if r.returncode == 0:
            return (np.load(path + "c.npy"), np.load(path + "s.npy"))
    except Exception:
        pass
    lin = np.linspace(0.0, 1.0, HD // 4).astype(np.float32)
    inv = np.concatenate([np.float32(1.0 / 1024.0) ** lin,
                          np.zeros(HD // 4, np.float32)])
    th = np.arange(T, dtype=np.float32)[:, None] * inv[None, :]
    return np.cos(th.astype(np.float64)).astype(np.float32), \
        np.sin(th.astype(np.float64)).astype(np.float32)


def _host_prep(L0_attn_x, L0_attn_W_qkv, L0_attn_W_out, L0_attn_s,
               L0_ffn_gate_w, L0_ffn_up_w, L0_ffn_down_w,
               L0_ffn_gate_s, L0_ffn_up_s, L0_ffn_down_s):
    bf = ml_dtypes.bfloat16
    x = np.ascontiguousarray(np.asarray(L0_attn_x, np.float32).reshape(T, D))
    Wqkv = np.asarray(L0_attn_W_qkv, np.float32)
    Wout = np.asarray(L0_attn_W_out, np.float32)
    s_in = np.asarray(L0_attn_s, np.float32)
    gate_w = np.asarray(L0_ffn_gate_w, np.float32)
    up_w = np.asarray(L0_ffn_up_w, np.float32)
    down_w = np.asarray(L0_ffn_down_w, np.float32)
    gate_s = np.asarray(L0_ffn_gate_s, np.float32)
    up_s = np.asarray(L0_ffn_up_s, np.float32)
    down_s = np.asarray(L0_ffn_down_s, np.float32)

    shared = {}
    shared["wout_r"] = np.ascontiguousarray(
        Wout.reshape(16, 128, 16, 128).transpose(2, 1, 0, 3))
    shared["gw_r"] = np.ascontiguousarray(
        gate_w.reshape(NIT, 128, 16, 128).transpose(0, 3, 2, 1))
    shared["uw_r"] = np.ascontiguousarray(
        up_w.reshape(NIT, 128, 16, 128).transpose(0, 3, 2, 1))
    shared["dw_r"] = np.ascontiguousarray(
        down_w.reshape(16, 128, NIT, 128).transpose(0, 3, 2, 1))
    shared["gsc"] = np.ascontiguousarray(gate_s.reshape(NIT, 128).T)
    shared["usc"] = np.ascontiguousarray(up_s.reshape(NIT, 128).T)
    shared["dsc"] = np.ascontiguousarray(down_s.reshape(16, 128).T)
    PS = np.zeros((128, 128), np.float32)
    for b in range(2):
        B = 64 * b
        for i in range(32):
            PS[B + 32 + i, B + i] = 1.0
            PS[B + i, B + 32 + i] = -1.0
    shared["pswap"] = PS.astype(bf)
    shared["ident"] = np.eye(128, dtype=bf)
    shared["onesr"] = np.ones((1, 128), np.float32)
    shared["konst"] = np.broadcast_to(
        np.array([1e-5, -0.5, 0.125, 0.0], np.float32), (128, 4)).copy()

    cos, sin = _rope_tables()
    shared["cosg"] = np.ascontiguousarray(np.tile(cos.T, (4, 1)))
    shared["sing"] = np.ascontiguousarray(np.tile(sin.T, (4, 1)))

    tk = np.arange(128)[:, None]
    xx = np.arange(2432)[None, :]
    bad = ((xx - 896 - tk) >= 0) & ((xx - 896 - tk) <= 511)
    shared["masks"] = np.ascontiguousarray(
        np.where(bad, 0.0, 1.0).astype(bf))

    in_maps = []
    for c in range(NCORES):
        m = dict(shared)
        xc = np.ascontiguousarray(x[TC * c:TC * (c + 1)])
        m["x_c"] = xc
        m["xT_c"] = np.ascontiguousarray(xc.T)
        wk = Wqkv[:, 2048 + 64 * c:2048 + 64 * (c + 1)]
        wv = Wqkv[:, 2560 + 64 * c:2560 + 64 * (c + 1)]
        m["wkv_r"] = np.ascontiguousarray(
            np.concatenate([wk, wv], axis=1).reshape(16, 128, 128))
        m["wq_r"] = np.ascontiguousarray(
            Wqkv[:, 256 * c:256 * (c + 1)].reshape(16, 128, 256))
        m["sink"] = np.ascontiguousarray(s_in.reshape(NKV, QM)[c:c + 1, :])
        in_maps.append(m)
    return in_maps


def kernel(**inputs):
    in_maps = _host_prep(**inputs)
    nc = build_nc()
    res = run_bass_kernel_spmd(nc, in_maps, core_ids=list(range(NCORES)))
    out = np.concatenate(
        [np.asarray(r["out_d"], np.float32).T for r in res.results], axis=0)
    return out.reshape(1, T, D)
